# version 03
# baseline (speedup 1.0000x reference)
import os
import math
import contextlib
from collections import defaultdict

import numpy as np
import ml_dtypes

import concourse.bass as bass
import concourse.mybir as mybir
from concourse.bass_utils import run_bass_kernel_spmd

F32 = mybir.dt.float32
BF16 = mybir.dt.bfloat16
F16 = mybir.dt.float16
I32 = mybir.dt.int32
AF = mybir.ActivationFunctionType
ALU = mybir.AluOpType
AX = mybir.AxisListType

E = 2048
T = 4096
TOWN = 2048
NT = T // 128
NTO = TOWN // 128
NEC = E // 128
NPR = 16
IFF = 5632
NIT = IFF // 128
LAMBDA_INIT = 0.8 - 0.6 * math.exp(-0.3 * 12)
EPS_LN = 1e-5
EPS_SUB = 1e-5
EPS_RMS = 1e-6

_PHASES = int(os.environ.get("MK_PHASES", "99"))
_P3NPR = int(os.environ.get("MK_P3NPR", "16"))
_P3NQC = int(os.environ.get("MK_P3NQC", "8"))
_P3TR = int(os.environ.get("MK_P3TR", "1"))
_P3EP = int(os.environ.get("MK_P3EP", "1"))
_P3NKTP = int(os.environ.get("MK_P3NKTP", "16"))
_DBG = bool(int(os.environ.get("MK_DBG", "0")))

ENGS = ("sy", "pe", "ac", "dv", "gp")
SEM_OWNER = {"pe": "pe", "ac": "ac", "dv": "dv"}


class K:

    def __init__(self, nc):
        self.nc = nc
        self.prog = []
        self.cnt = defaultdict(int)
        self.sem = {}
        self.hw = {en: defaultdict(int) for en in ENGS}

    def step(self, eng, fn, waits=(), inc=None, incn=1):
        ws = []
        for (s, v) in waits:
            if v and v > 0:
                ws.append((s, v))
        if inc is not None:
            self.cnt[inc] += incn
        self.prog.append((eng, fn, tuple(ws), (inc, incn) if inc else None))
        return self.cnt[inc] if inc else None

    def dma(self, eng, fn, waits=(), sem=None):
        return self.step(eng, fn, waits=waits, inc=sem or eng, incn=16)

    def tag(self, inc, incn=1):
        eng, fn, ws, post = self.prog[-1]
        assert post is None, "last step already has an inc"
        self.cnt[inc] += incn
        self.prog[-1] = (eng, fn, ws, (inc, incn))
        return self.cnt[inc]

    def barrier(self):
        snap = [(s, c) for (s, c) in self.cnt.items() if c > 0]
        for eng in ENGS:
            self.step(eng, lambda e: e.nop(), waits=snap)

    def flush(self):
        if not self.prog:
            return
        prog, self.prog = self.prog, []
        by_eng = {en: [] for en in ENGS}
        for (e2, fn, ws, post) in prog:
            by_eng[e2].append((fn, ws, post))

        def mk(en):
            def run(e):
                hwe = self.hw[en]
                for (fn, ws, post) in by_eng[en]:
                    for (s, v) in ws:
                        if v > hwe[s]:
                            e.wait_ge(self.sem[s], v)
                            hwe[s] = v
                    ins = fn(e)
                    if post is not None:
                        s, n = post
                        ins.then_inc(self.sem[s], n)
            return run
        with self.nc.Block() as block:
            m = {"sy": block.sync, "pe": block.tensor, "ac": block.scalar,
                 "dv": block.vector, "gp": block.gpsimd}
            for en in ENGS:
                if by_eng[en]:
                    m[en](mk(en))


def build(phases=99, dbg=False):
    nc = bass.Bass("TRN2", target_bir_lowering=False)
    k = K(nc)

    xb = nc.dram_tensor("xb", [T, E], F32, kind="ExternalInput")
    wqT = nc.dram_tensor("wqT", [E, E], F32, kind="ExternalInput")
    wkT = nc.dram_tensor("wkT", [E, E], F32, kind="ExternalInput")
    wvT = nc.dram_tensor("wvT", [E, E], F32, kind="ExternalInput")
    woT = nc.dram_tensor("woT", [E, E], F32, kind="ExternalInput")
    ropeC = nc.dram_tensor("ropeC", [T, E], F32, kind="ExternalInput")
    ropeS = nc.dram_tensor("ropeS", [T, E], F32, kind="ExternalInput")
    lvec = nc.dram_tensor("lvec", [1, 256], F32, kind="ExternalInput")
    swn = nc.dram_tensor("swn", [128, 1], F32, kind="ExternalInput")
    gup = nc.dram_tensor("gup", [2 * IFF, 256], I32, kind="ExternalInput")
    gus = nc.dram_tensor("gus", [2 * IFF, 16], F16, kind="ExternalInput")
    dnp = nc.dram_tensor("dnp", [E, 704], I32, kind="ExternalInput")
    dns = nc.dram_tensor("dns", [E, 44], F16, kind="ExternalInput")
    ident = nc.dram_tensor("ident", [128, 128], BF16, kind="ExternalInput")

    out = nc.dram_tensor("out", [TOWN, E], F32, kind="ExternalOutput")

    skind = "ExternalOutput" if dbg else "Internal"
    xhT_s = nc.dram_tensor("xhT_s", [NEC, 128, T], BF16, kind=skind)
    kT_s = nc.dram_tensor("kT_s", [NPR, 128, T], BF16, kind=skind)
    v_s = nc.dram_tensor("v_s", [NPR, T, 128], BF16, kind=skind)
    qT_s = nc.dram_tensor("qT_s", [NPR, 128, TOWN], BF16, kind=skind)
    x1_s = nc.dram_tensor("x1_s", [TOWN, E], F32, kind=skind)
    hT_s = nc.dram_tensor("hT_s", [NIT, 128, TOWN], BF16, kind=skind)
    y_s = nc.dram_tensor("y_s", [TOWN, E], F32, kind=skind)
    dbg_aT = nc.dram_tensor("dbg_aT", [128, NPR * TOWN], BF16, kind=skind)
    dbg_mb = nc.dram_tensor("dbg_mb", [1, 8], F32, kind=skind)

    with contextlib.ExitStack() as ctx:
        for s in ("sy", "gp", "pe", "ac", "dv", "st"):
            k.sem[s] = ctx.enter_context(nc.semaphore(s))

        consts = ctx.enter_context(nc.sbuf_tensor("consts", [128, 8], F32))
        onesf = ctx.enter_context(nc.sbuf_tensor("onesf", [1, 128], F32))
        identb = ctx.enter_context(nc.sbuf_tensor("identb", [128, 128], BF16))
        wcol = ctx.enter_context(nc.sbuf_tensor("wcol", [128, 1], F32))
        nlamcol = ctx.enter_context(nc.sbuf_tensor("nlamcol", [128, 1], F32))
        mbcol = ctx.enter_context(nc.sbuf_tensor("mbcol", [128, 1], F32))
        smax = ctx.enter_context(nc.sbuf_tensor("smax", [128, 2], F32))
        smaxb = ctx.enter_context(nc.sbuf_tensor("smaxb", [128, 2], BF16))
        small = ctx.enter_context(nc.sbuf_tensor("small", [128, 16], F32))
        small2 = ctx.enter_context(nc.sbuf_tensor("small2", [128, 8], F32))
        tiny = ctx.enter_context(nc.sbuf_tensor("tiny", [1, 384], F32))
        ysq = ctx.enter_context(nc.sbuf_tensor("ysq", [128, 16], F32))

        c_m4, c_m6 = consts[:, 0:1], consts[:, 1:2]
        c_eln, c_esub, c_erms = consts[:, 2:3], consts[:, 3:4], consts[:, 4:5]
        nc.const_aps.register(F32, 1.0, consts[:, 5:6])

        k.step("gp", lambda e: e.memset(consts[:, 0:1], -4.0))
        k.step("gp", lambda e: e.memset(consts[:, 1:2], -6.0))
        k.step("gp", lambda e: e.memset(consts[:, 2:3], EPS_LN))
        k.step("gp", lambda e: e.memset(consts[:, 3:4], EPS_SUB))
        k.step("gp", lambda e: e.memset(consts[:, 4:5], EPS_RMS))
        k.step("gp", lambda e: e.memset(consts[:, 5:6], 1.0))
        k.step("gp", lambda e: e.memset(smax[:, :], 0.0))
        k.step("gp", lambda e: e.memset(onesf[:, :], 1.0), inc="gp", incn=1)
        k.dma("sy", lambda e: e.dma_start(identb[:, :], ident[:, :]))
        k.dma("sy", lambda e: e.dma_start(wcol[:, :], swn[:, :]))
        k.dma("sy", lambda e: e.dma_start(tiny[0:1, 0:256], lvec[:, :]))
        k.barrier()

        ctx12 = ctx.enter_context(contextlib.ExitStack())
        xhT_sb = ctx12.enter_context(nc.sbuf_tensor("xhT_sb", [128, NEC * T], BF16))
        if phases >= 1:
            with contextlib.ExitStack() as p1:
                sx = [p1.enter_context(nc.sbuf_tensor(f"p1x{i}", [128, E], F32)) for i in range(2)]
                sqd = p1.enter_context(nc.sbuf_tensor("p1sqd", [128, E], F32))
                xh = p1.enter_context(nc.sbuf_tensor("p1xh", [128, E], BF16))
                ptr = [p1.enter_context(nc.psum_tensor(f"p1pt{i}", [128, 512], BF16))
                       for i in range(2)]
                ld = [0, 0]
                ev = [0, 0]
                used = [0, 0]
                xh_users = 0
                prev_var = 0
                for t in range(NT):
                    par = t % 2
                    w = [("dv", used[par])] if used[par] else []
                    ld[par] = k.dma("sy", lambda e, t=t, par=par: e.dma_start(
                        sx[par][:, :], xb[t * 128:(t + 1) * 128, :]), waits=w)
                    c_a = k.step("dv", lambda e, par=par: e.reduce_sum(
                        small[:, 0:1], sx[par][:, :], AX.X), waits=[("sy", ld[par])],
                        inc="dv")
                    c_sq = k.step("ac", lambda e, par=par: e.activation(
                        sqd[:, :], sx[par][:, :], AF.Square, accum_out=small[:, 1:2]),
                        waits=[("sy", ld[par]), ("dv", prev_var)], inc="ac")
                    c_b = k.step("dv", lambda e: e.tensor_scalar(
                        small[:, 2:3], small[:, 0:1], 1.0 / E, None, ALU.mult),
                        waits=[("dv", c_a)], inc="dv")
                    c_c = k.step("dv", lambda e: e.tensor_mul(
                        small[:, 3:4], small[:, 2:3], small[:, 2:3]),
                        waits=[("dv", c_b)], inc="dv")
                    c_var = k.step("dv", lambda e: e.tensor_scalar(
                        small[:, 4:5], small[:, 1:2], 1.0 / E, small[:, 3:4],
                        ALU.mult, ALU.subtract),
                        waits=[("ac", c_sq), ("dv", c_c)], inc="dv")
                    prev_var = c_var
                    c_ln = k.step("ac", lambda e: e.activation(
                        small[:, 5:6], small[:, 4:5], AF.Ln, bias=c_eln),
                        waits=[("dv", c_var)], inc="ac")
                    c_rs = k.step("ac", lambda e: e.activation(
                        small[:, 6:7], small[:, 5:6], AF.Exp, scale=-0.5),
                        waits=[("ac", c_ln)], inc="ac")
                    c_xh = k.step("dv", lambda e, par=par: e.tensor_scalar(
                        xh[:, :], sx[par][:, :], small[:, 2:3], small[:, 6:7],
                        ALU.subtract, ALU.mult),
                        waits=[("ac", c_rs), ("pe", xh_users)], inc="dv")
                    used[par] = c_xh
                    for g in range(4):
                        pp = g % 2
                        for j in range(4):
                            ec = g * 4 + j
                            w = [("dv", c_xh)]
                            if ev[pp]:
                                w.append(("dv", ev[pp]))
                            k.step("pe", lambda e, ec=ec, pp=pp, j=j: e.transpose(
                                ptr[pp][:, j * 128:(j + 1) * 128],
                                xh[:, ec * 128:(ec + 1) * 128], identb[:, :]),
                                waits=w if j == 0 else [("dv", c_xh)])
                        c_tr = k.tag("pe")
                        if g == 3:
                            xh_users = c_tr
                        dst = bass.AP(xhT_sb, (g * 4) * T + t * 128,
                                      [[NEC * T, 128], [T, 4], [1, 128]])
                        ev[pp] = k.step("dv", lambda e, pp=pp, dst=dst: e.tensor_copy(
                            dst, ptr[pp][:, :]), waits=[("pe", c_tr)], inc="dv")
                if dbg:
                    k.dma("gp", lambda e: e.dma_start(
                        xhT_s[:, :, :].transpose([1, 0, 2]),
                        xhT_sb[:, :].rearrange("p (ec t) -> p ec t", ec=NEC)),
                        waits=[("dv", k.cnt["dv"])])
                k.barrier()
                k.flush()

        if phases >= 2:
            with contextlib.ExitStack() as p2:
                wbf = [p2.enter_context(nc.sbuf_tensor(f"p2w{i}", [128, NEC * 512], BF16))
                       for i in range(2)]
                tbc = [p2.enter_context(nc.sbuf_tensor(f"p2tc{i}", [128, 512], F32))
                       for i in range(2)]
                tbs = [p2.enter_context(nc.sbuf_tensor(f"p2ts{i}", [128, 512], F32))
                       for i in range(2)]
                tmp1 = p2.enter_context(nc.sbuf_tensor("p2t1", [128, 512], F32))
                tmp2 = p2.enter_context(nc.sbuf_tensor("p2t2", [128, 512], F32))
                rost = [p2.enter_context(nc.sbuf_tensor(f"p2ro{i}", [128, 512], BF16))
                        for i in range(2)]
                sqn = p2.enter_context(nc.sbuf_tensor("p2sqn", [128, 512], F32))
                stg = [p2.enter_context(nc.sbuf_tensor(f"p2st{i}", [128, 512], BF16))
                       for i in range(2)]
                pj = [p2.enter_context(nc.psum_tensor(f"p2p{i}", [128, 512], F32))
                      for i in range(2)]
                ptr = [p2.enter_context(nc.psum_tensor(f"p2pt{i}", [128, 512], BF16))
                       for i in range(2)]

                projs = [("k", wkT, True, NT, 0),
                         ("q", wqT, True, NTO, 1),
                         ("v", wvT, False, NT, None)]
                chunks = [(pi, dc) for pi in range(3) for dc in range(4)]
                wload = {}
                mmlast = {}
                for ci in range(2):
                    pi, dc = chunks[ci]
                    wT = projs[pi][1]
                    wload[ci] = k.dma("gp", lambda e, wT=wT, dc=dc, ci=ci: e.dma_start(
                        wbf[ci % 2][:, :].rearrange("p (ec d) -> p ec d", d=512),
                        wT[:, dc * 512:(dc + 1) * 512].rearrange("(ec p) d -> p ec d", p=128)))

                tb_user = [0, 0]
                pj_free = [None, None]
                ptr_free = [None, None]
                stg_free = [0, 0]
                for ci, (pi, dc) in enumerate(chunks):
                    name, wT, doro, ntt, sm_i = projs[pi]
                    wpar = ci % 2
                    for t in range(ntt):
                        par = t % 2
                        if doro:
                            w = [("dv", tb_user[par])] if tb_user[par] else []
                            k.dma("sy", lambda e, t=t, dc=dc, par=par: e.dma_start(
                                tbc[par][:, :],
                                ropeC[t * 128:(t + 1) * 128, dc * 512:(dc + 1) * 512]),
                                waits=w)
                            tbl = k.dma("sy", lambda e, t=t, dc=dc, par=par: e.dma_start(
                                tbs[par][:, :],
                                ropeS[t * 128:(t + 1) * 128, dc * 512:(dc + 1) * 512]))
                        for ec in range(NEC):
                            w = []
                            if ec == 0:
                                w = [("gp", wload[ci])]
                                if pj_free[par]:
                                    w.append(pj_free[par])
                            k.step("pe", lambda e, ec=ec, t=t, par=par, wpar=wpar,
                                   first=(ec == 0), last=(ec == NEC - 1): e.matmul(
                                       pj[par][:, :],
                                       xhT_sb[:, ec * T + t * 128: ec * T + (t + 1) * 128],
                                       wbf[wpar][:, ec * 512:(ec + 1) * 512],
                                       start=first, stop=last), waits=w)
                        mmc = k.tag("pe")
                        if t == ntt - 1:
                            mmlast[ci] = mmc
                        if doro:
                            k.step("dv", lambda e, par=par: e.tensor_mul(
                                tmp1[:, :], pj[par][:, :], tbc[par][:, :]),
                                waits=[("pe", mmc), ("sy", tbl)])
                            c2 = k.step("dv", lambda e, par=par: e.tensor_mul(
                                tmp2[:, :], pj[par][:, :], tbs[par][:, :]), inc="dv")
                            pj_free[par] = ("dv", c2)
                            tb_user[par] = c2
                            t1e = bass.AP(tmp1, 0, [[512, 128], [2, 256]])
                            t1o = bass.AP(tmp1, 1, [[512, 128], [2, 256]])
                            t2e = bass.AP(tmp2, 0, [[512, 128], [2, 256]])
                            t2o = bass.AP(tmp2, 1, [[512, 128], [2, 256]])
                            roe = bass.AP(rost[par], 0, [[512, 128], [2, 256]])
                            roo = bass.AP(rost[par], 1, [[512, 128], [2, 256]])
                            w = [ptr_free[par]] if ptr_free[par] else []
                            k.step("dv", lambda e, a=roe, b=t1e, c=t2o: e.tensor_sub(a, b, c),
                                   waits=w)
                            c_ro = k.step("dv", lambda e, a=roo, b=t2e, c=t1o:
                                          e.tensor_add(a, b, c), inc="dv")
                            k.step("dv", lambda e, par=par: e.tensor_mul(
                                sqn[:, :], rost[par][:, :], rost[par][:, :]))
                            n1 = k.step("dv", lambda e: e.tensor_reduce(
                                small2[:, 0:8],
                                sqn[:, :].rearrange("p (h d) -> p h d", h=8),
                                AX.X, ALU.add), inc="dv")
                            n2 = k.step("dv", lambda e: e.tensor_reduce(
                                small[:, 7:8], small2[:, 0:8], AX.X, ALU.max),
                                waits=[("dv", n1)], inc="dv")
                            k.step("dv", lambda e, sm_i=sm_i: e.tensor_tensor(
                                smax[:, sm_i:sm_i + 1], smax[:, sm_i:sm_i + 1],
                                small[:, 7:8], ALU.max), waits=[("dv", n2)])
                            for j in range(4):
                                w = [("dv", c_ro)]
                                if j == 0 and stg_free[par]:
                                    w.append(("gp", stg_free[par]))
                                k.step("pe", lambda e, j=j, par=par: e.transpose(
                                    ptr[par][:, j * 128:(j + 1) * 128],
                                    rost[par][:, j * 128:(j + 1) * 128], identb[:, :]),
                                    waits=w)
                            c_tr = k.tag("pe")
                            c_ev = k.step("ac", lambda e, par=par: e.copy(
                                stg[par][:, :], ptr[par][:, :]),
                                waits=[("pe", c_tr)], inc="ac")
                            ptr_free[par] = ("ac", c_ev)
                            dst_s = kT_s if name == "k" else qT_s
                            dlen = T if name == "k" else TOWN
                            dst = bass.AP(dst_s, (4 * dc) * 128 * dlen + t * 128,
                                          [[dlen, 128], [128 * dlen, 4], [1, 128]])
                            stg_free[par] = k.dma(
                                "gp", lambda e, par=par, dst=dst: e.dma_start(
                                    dst, stg[par][:, :].rearrange(
                                        "p (pr d) -> p pr d", pr=4)),
                                waits=[("ac", c_ev)])
                        else:
                            w = [("pe", mmc)]
                            if stg_free[par]:
                                w.append(("gp", stg_free[par]))
                            c_ev = k.step("ac", lambda e, par=par: e.copy(
                                stg[par][:, :], pj[par][:, :]), waits=w, inc="ac")
                            pj_free[par] = ("ac", c_ev)
                            dst = bass.AP(v_s, (4 * dc) * T * 128 + t * 128 * 128,
                                          [[128, 128], [T * 128, 4], [1, 128]])
                            stg_free[par] = k.dma(
                                "gp", lambda e, par=par, dst=dst: e.dma_start(
                                    dst, stg[par][:, :].rearrange(
                                        "p (pr d) -> p pr d", pr=4)),
                                waits=[("ac", c_ev)])
                    if ci + 2 < len(chunks):
                        pi2, dc2 = chunks[ci + 2]
                        wT2 = projs[pi2][1]
                        wload[ci + 2] = k.dma(
                            "gp", lambda e, wT2=wT2, dc2=dc2, ci=ci: e.dma_start(
                                wbf[ci % 2][:, :].rearrange("p (ec d) -> p ec d", d=512),
                                wT2[:, dc2 * 512:(dc2 + 1) * 512].rearrange(
                                    "(ec p) d -> p ec d", p=128)),
                            waits=[("pe", mmlast[ci])])

                c1 = k.step("dv", lambda e: e.tensor_copy(smaxb[:, :], smax[:, :]), inc="dv")
                k.step("pe", lambda e: e.transpose(
                    bass.AP(ptr[0], 0, [[512, 1], [1, 128]]),
                    smaxb[:, 0:1], identb[:, :]), waits=[("dv", c1)])
                k.step("pe", lambda e: e.transpose(
                    bass.AP(ptr[0], 128, [[512, 1], [1, 128]]),
                    smaxb[:, 1:2], identb[:, :]))
                c2 = k.tag("pe")
                k.step("dv", lambda e: e.tensor_reduce(
                    tiny[0:1, 256:257], bass.AP(ptr[0], 0, [[512, 1], [1, 128]]),
                    AX.X, ALU.max), waits=[("pe", c2)])
                c3 = k.step("dv", lambda e: e.tensor_reduce(
                    tiny[0:1, 257:258], bass.AP(ptr[0], 128, [[512, 1], [1, 128]]),
                    AX.X, ALU.max), inc="dv")
                c4a = k.step("ac", lambda e: e.activation(
                    tiny[0:1, 258:259], tiny[0:1, 256:257], AF.Ln),
                    waits=[("dv", c3)], inc="ac")
                c4 = k.step("ac", lambda e: e.activation(
                    tiny[0:1, 259:260], tiny[0:1, 257:258], AF.Ln),
                    waits=[("ac", c4a)], inc="ac")
                c5 = k.step("dv", lambda e: e.tensor_add(
                    tiny[0:1, 260:261], tiny[0:1, 258:259], tiny[0:1, 259:260]),
                    waits=[("ac", c4)], inc="dv")
                c6 = k.step("ac", lambda e: e.activation(
                    tiny[0:1, 261:262], tiny[0:1, 260:261], AF.Exp, scale=0.5),
                    waits=[("dv", c5)], inc="ac")
                c7 = k.step("dv", lambda e: e.tensor_scalar(
                    tiny[0:1, 262:263], tiny[0:1, 261:262], -1.0, 32.0,
                    ALU.mult, ALU.add), waits=[("ac", c6)], inc="dv")
                l1 = k.step("dv", lambda e: e.tensor_mul(
                    tiny[0:1, 264:328], tiny[0:1, 0:64], tiny[0:1, 64:128]), inc="dv")
                l2 = k.step("dv", lambda e: e.reduce_sum(
                    tiny[0:1, 328:329], tiny[0:1, 264:328], AX.X),
                    waits=[("dv", l1)], inc="dv")
                l3 = k.step("dv", lambda e: e.tensor_mul(
                    tiny[0:1, 264:328], tiny[0:1, 128:192], tiny[0:1, 192:256]),
                    waits=[("dv", l2)], inc="dv")
                c8 = k.step("dv", lambda e: e.reduce_sum(
                    tiny[0:1, 329:330], tiny[0:1, 264:328], AX.X),
                    waits=[("dv", l3)], inc="dv")
                l4 = k.step("ac", lambda e: e.activation(
                    tiny[0:1, 330:331], tiny[0:1, 328:329], AF.Exp),
                    waits=[("dv", c8)], inc="ac")
                c9 = k.step("ac", lambda e: e.activation(
                    tiny[0:1, 331:332], tiny[0:1, 329:330], AF.Exp),
                    waits=[("ac", l4)], inc="ac")
                l5 = k.step("dv", lambda e: e.tensor_sub(
                    tiny[0:1, 332:333], tiny[0:1, 330:331], tiny[0:1, 331:332]),
                    waits=[("ac", c9)], inc="dv")
                c10 = k.step("dv", lambda e: e.tensor_scalar(
                    tiny[0:1, 333:334], tiny[0:1, 332:333], -1.0,
                    LAMBDA_INIT, ALU.mult, ALU.subtract),
                    waits=[("dv", l5)], inc="dv")
                k.step("pe", lambda e: e.matmul(
                    pj[0][:, 0:1], onesf[:, :], tiny[0:1, 262:263],
                    start=True, stop=True), waits=[("dv", c7), ("dv", c10)])
                c11 = k.step("pe", lambda e: e.matmul(
                    pj[0][:, 4:5], onesf[:, :], tiny[0:1, 333:334],
                    start=True, stop=True), inc="pe")
                k.step("dv", lambda e: e.tensor_copy(mbcol[:, :], pj[0][:, 0:1]),
                       waits=[("pe", c11)])
                k.step("dv", lambda e: e.tensor_copy(nlamcol[:, :], pj[0][:, 4:5]),
                       inc="dv")
                if dbg:
                    k.dma("gp", lambda e: e.dma_start(dbg_mb[:, :], tiny[0:1, 256:264]),
                          waits=[("dv", k.cnt["dv"])])
                k.barrier()
                k.flush()
        ctx12.close()

        ctx34 = ctx.enter_context(contextlib.ExitStack())
        atT_sb = ctx34.enter_context(nc.sbuf_tensor("atT_sb", [128, NPR * TOWN], BF16))
        if phases >= 3:
            with contextlib.ExitStack() as p3:
                SAps = p3.enter_context(nc.psum_tensor("p3SA", [128, 1024], F32))
                SBps = p3.enter_context(nc.psum_tensor("p3SB", [128, 1024], F32))
                U4 = p3.enter_context(nc.psum_tensor("p3U", [128, 2048], F32))
                Ebuf = [p3.enter_context(nc.sbuf_tensor(f"p3E{i}", [128, 1024], BF16))
                        for i in range(2)]
                ksl = [p3.enter_context(nc.sbuf_tensor(f"p3k{i}", [128, T], BF16))
                       for i in range(2)]
                vsl = [p3.enter_context(nc.sbuf_tensor(f"p3v{i}", [128, 32 * 129], BF16))
                       for i in range(2)]
                qsl = [p3.enter_context(nc.sbuf_tensor(f"p3q{i}", [128, TOWN], BF16))
                       for i in range(2)]
                astg = p3.enter_context(nc.sbuf_tensor("p3as", [128, TOWN], BF16))
                ubuf = p3.enter_context(nc.sbuf_tensor("p3ub", [128, 516], F32))
                ga = p3.enter_context(nc.sbuf_tensor("p3ga", [128, 256], F32))
                gb = p3.enter_context(nc.sbuf_tensor("p3gb", [128, 256], F32))
                Sgrp = [SAps, SBps]
                SAbf = SAps[:, :].bitcast(BF16)
                SBbf = SBps[:, :].bitcast(BF16)

                for i in range(2):
                    k.step("gp", lambda e, i=i: e.memset(
                        bass.AP(vsl[i], 128, [[32 * 129, 128], [129, 32]]), 1.0),
                        inc="gp", incn=1)

                slab_ld = {}

                def load_slabs(pr, wpe):
                    par = pr % 2
                    w = [("pe", wpe)] if wpe else []
                    k.dma("sy", lambda e, pr=pr, par=par: e.dma_start(
                        ksl[par][:, :], kT_s[pr, :, :]), waits=w)
                    k.dma("sy", lambda e, pr=pr, par=par: e.dma_start(
                        qsl[par][:, :], qT_s[pr, :, :]))
                    dstv = bass.AP(vsl[par], 0, [[32 * 129, 128], [129, 32], [1, 128]])
                    slab_ld[pr] = k.dma("sy", lambda e, pr=pr, dstv=dstv: e.dma_start(
                        dstv, v_s[pr, :, :].rearrange("(kt p) d -> p kt d", p=128)))

                load_slabs(0, 0)
                NQC = _P3NQC
                UPP = 16 * NQC
                NU = _P3NPR * UPP

                def uidx(i):
                    return i // UPP, (i % UPP) // 16, i % 16

                exp_hist = {}
                av_hist = {}
                qk_hist = {}
                pr_lastav = {-1: 0}
                ep_done = 0
                ep_chain = 0
                tr_evac = 0

                def emit_qk(i):
                    pr, qc, ktp = uidx(i)
                    spar = i % 2
                    S = Sgrp[spar]
                    par = pr % 2
                    w = [("sy", slab_ld[pr])]
                    if i - 2 in exp_hist:
                        w.append(("ac", exp_hist[i - 2][1]))
                    if i % UPP == 1 and tr_evac:
                        w.append(("dv", tr_evac))
                    tags = []
                    for sub in range(2):
                        kt = 2 * ktp + sub
                        for h01 in range(2):
                            ww = w if (sub == 0 and h01 == 0) else ()
                            k.step("pe", lambda e, S=S, sub=sub, h01=h01, kt=kt,
                                   par=par, qc=qc: e.matmul(
                                       S[:, h01 * 512 + sub * 256:
                                         h01 * 512 + sub * 256 + 256],
                                       ksl[par][h01 * 64:(h01 + 1) * 64,
                                                kt * 128:(kt + 1) * 128],
                                       qsl[par][h01 * 64:(h01 + 1) * 64,
                                                qc * 256:(qc + 1) * 256],
                                       start=True, stop=True), waits=ww)
                        tags.append(k.tag("pe"))
                    qk_hist[i] = tuple(tags)

                emit_qk(0)
                for i in range(NU):
                    pr, qc, ktp = uidx(i)
                    par = pr % 2
                    spar = i % 2
                    S = Sgrp[spar]
                    epar = i % 2
                    etags = []
                    for sub in range(2):
                        sview = bass.AP(S, sub * 256, [[1024, 128], [512, 2], [1, 256]])
                        eview = bass.AP(Ebuf[epar], sub * 256,
                                        [[1024, 128], [512, 2], [1, 256]])
                        w = [("pe", qk_hist[i][sub])]
                        if sub == 0 and i - 2 in av_hist:
                            w.append(("pe", av_hist[i - 2]))
                        etags.append(k.step(
                            "ac", lambda e, sv=sview, ev=eview: e.activation(
                                ev, sv, AF.Exp, bias=mbcol[:, 0:1]),
                            waits=w, inc="ac"))
                    exp_hist[i] = tuple(etags)
                    if i % UPP == 16 * (NQC // 2) and pr + 1 < _P3NPR:
                        load_slabs(pr + 1, pr_lastav[pr - 1])
                    if i + 1 < NU:
                        emit_qk(i + 1)
                    for sub in range(2):
                        kt = 2 * ktp + sub
                        w = [("ac", exp_hist[i][sub])]
                        if sub == 0 and ktp == 0 and ep_done:
                            w.append(("dv", ep_done))
                        first_mm = True
                        for h01 in range(2):
                            for qs in range(2):
                                ww = w if first_mm else ()
                                first_mm = False
                                ucol = (h01 * 2 + qs) * 512
                                eoff = h01 * 512 + sub * 256 + qs * 128
                                k.step("pe", lambda e, epar=epar, kt=kt,
                                       par=par, ucol=ucol, eoff=eoff: e.matmul(
                                           U4[:, ucol:ucol + 129],
                                           Ebuf[epar][:, eoff:eoff + 128],
                                           vsl[par][:, kt * 129:kt * 129 + 129],
                                           start=(kt == 0), stop=(kt == 31)),
                                       waits=ww)
                    av_hist[i] = k.tag("pe")
                    if ktp != 15:
                        continue
                    avl = av_hist[i]
                    if _P3EP:
                        usrc = bass.AP(U4, 0, [[2048, 128], [512, 4], [1, 129]])
                        ep_done = k.step("dv", lambda e, usrc=usrc: e.tensor_copy(
                            ubuf[:, :].rearrange("p (j c) -> p j c", j=4), usrc),
                            waits=[("pe", avl)], inc="dv")
                        ssrc = bass.AP(ubuf, 128, [[516, 128], [129, 4]])
                        h1 = k.step("dv", lambda e, ssrc=ssrc: e.reciprocal(
                            small[:, 8:12], ssrc), waits=[("dv", ep_done)], inc="dv")
                        nlv = bass.AP(nlamcol, 0, [[1, 128], [0, 2]])
                        h2 = k.step("dv", lambda e, nlv=nlv: e.tensor_tensor(
                            small[:, 12:14], small[:, 10:12], nlv, ALU.mult),
                            waits=[("dv", h1)], inc="dv")
                        u0v = bass.AP(ubuf, 0, [[516, 128], [129, 2], [1, 128]])
                        u1v = bass.AP(ubuf, 2 * 129, [[516, 128], [129, 2], [1, 128]])
                        r0bc = bass.AP(small, 8, [[16, 128], [1, 2], [0, 128]])
                        nlbc = bass.AP(small, 12, [[16, 128], [1, 2], [0, 128]])
                        h3 = k.step("dv", lambda e, u0v=u0v, r0bc=r0bc: e.tensor_tensor(
                            ga[:, :].rearrange("p (j c) -> p j c", j=2), u0v, r0bc,
                            ALU.mult), waits=[("dv", h2)], inc="dv")
                        h4 = k.step("dv", lambda e, u1v=u1v, nlbc=nlbc: e.tensor_tensor(
                            gb[:, :].rearrange("p (j c) -> p j c", j=2), u1v, nlbc,
                            ALU.mult), waits=[("dv", h3)], inc="dv")
                        h5 = k.step("dv", lambda e: e.tensor_add(
                            ga[:, :], ga[:, :], gb[:, :]), waits=[("dv", h4)], inc="dv")
                        h6 = k.step("dv", lambda e: e.tensor_mul(
                            gb[:, :], ga[:, :], ga[:, :]), waits=[("dv", h5)], inc="dv")
                        h7 = k.step("dv", lambda e: e.tensor_reduce(
                            small[:, 14:16], gb[:, :].rearrange("p (j c) -> p j c", j=2),
                            AX.X, ALU.add), waits=[("dv", h6)], inc="dv")
                        h8 = k.step("ac", lambda e: e.activation(
                            small[:, 4:6], small[:, 14:16], AF.Ln,
                            bias=c_esub, scale=1.0 / 128), waits=[("dv", h7)], inc="ac")
                        h9 = k.step("ac", lambda e: e.activation(
                            small[:, 6:8], small[:, 4:6], AF.Exp, scale=-0.5),
                            waits=[("ac", h8)], inc="ac")
                        rsbc = bass.AP(small, 6, [[16, 128], [1, 2], [0, 128]])
                        ep_chain = k.step(
                            "dv", lambda e, qc=qc, rsbc=rsbc: e.tensor_tensor(
                                astg[:, qc * 256:(qc + 1) * 256].rearrange(
                                    "p (j c) -> p j c", j=2),
                                ga[:, :].rearrange("p (j c) -> p j c", j=2),
                                rsbc, ALU.mult), waits=[("ac", h9)], inc="dv")
                    if qc != NQC - 1:
                        continue
                    pr_lastav[pr] = av_hist[i]
                    lastexp = exp_hist[i][1]
                    for j in range(NTO if _P3TR else 0):
                        w = [("dv", ep_chain)] if j > 0 else [("dv", ep_chain),
                                                              ("ac", lastexp)]
                        k.step("pe", lambda e, j=j: e.transpose(
                            SBbf[:, j * 128:(j + 1) * 128],
                            astg[:, j * 128:(j + 1) * 128], identb[:, :]), waits=w)
                    if _P3TR:
                        c_tr = k.tag("pe")
                        tr_evac = k.step("dv", lambda e, pr=pr: e.tensor_copy(
                            atT_sb[:, pr * TOWN:(pr + 1) * TOWN], SBbf[:, 0:TOWN]),
                            waits=[("pe", c_tr)], inc="dv")
                if dbg:
                    k.dma("gp", lambda e: e.dma_start(dbg_aT[:, :], atT_sb[:, :]),
                          waits=[("dv", tr_evac)])
                k.barrier()
                k.flush()

        if phases >= 4:
            with contextlib.ExitStack() as p4:
                wouts = p4.enter_context(nc.sbuf_tensor("p4w", [128, NEC * E], BF16))
                wtmp = [p4.enter_context(nc.sbuf_tensor(f"p4wt{i}", [128, E], F32))
                        for i in range(2)]
                xt = [p4.enter_context(nc.sbuf_tensor(f"p4x{i}", [128, E], F32))
                      for i in range(2)]
                x1t = [p4.enter_context(nc.sbuf_tensor(f"p4x1{i}", [128, E], F32))
                       for i in range(2)]
                sqd = p4.enter_context(nc.sbuf_tensor("p4sqd", [128, E], F32))
                xh2 = p4.enter_context(nc.sbuf_tensor("p4xh2", [128, E], BF16))
                x2stg = [p4.enter_context(nc.sbuf_tensor(f"p4xs{i}", [128, E], BF16))
                         for i in range(2)]
                PY = p4.enter_context(nc.psum_tensor("p4py", [128, 2048], F32))
                PT = [p4.enter_context(nc.psum_tensor(f"p4pt{i}", [128, 512], BF16))
                      for i in range(2)]
                wl = {}
                for dc in range(NEC):
                    par = dc % 2
                    w = [("ac", wl[dc - 2])] if dc >= 2 else []
                    c_l = k.dma("sy", lambda e, dc=dc, par=par: e.dma_start(
                        wtmp[par][:, :], woT[dc * 128:(dc + 1) * 128, :]), waits=w)
                    wl[dc] = k.step("ac", lambda e, dc=dc, par=par: e.activation(
                        wouts[:, dc * E:(dc + 1) * E], wtmp[par][:, :], AF.Copy,
                        scale=wcol[:, 0:1]), waits=[("sy", c_l)], inc="ac")
                wready = wl[NEC - 1]
                py_free = 0
                x1_free = [0, 0]
                xh_users = 0
                prev_var4 = 0
                ev = [0, 0]
                stg_user = [0, 0]
                for t in range(NTO):
                    par = t % 2
                    w = [("dv", x1_free[par])] if x1_free[par] else []
                    c_x = k.dma("sy", lambda e, t=t, par=par: e.dma_start(
                        xt[par][:, :], xb[t * 128:(t + 1) * 128, :]), waits=w)
                    for ec in range(4):
                        for dv_ in range(NEC):
                            w = []
                            if ec == 0 and dv_ == 0:
                                w = [("ac", wready)]
                                if py_free:
                                    w.append(("dv", py_free))
                            k.step("pe", lambda e, ec=ec, dv_=dv_, t=t: e.matmul(
                                PY[:, ec * 512:(ec + 1) * 512],
                                atT_sb[:, dv_ * TOWN + t * 128:dv_ * TOWN + (t + 1) * 128],
                                wouts[:, dv_ * E + ec * 512:dv_ * E + (ec + 1) * 512],
                                start=(dv_ == 0), stop=(dv_ == NEC - 1)), waits=w)
                    mmc = k.tag("pe")
                    py_free = k.step("dv", lambda e, par=par: e.tensor_add(
                        x1t[par][:, :], PY[:, :], xt[par][:, :]),
                        waits=[("pe", mmc), ("sy", c_x)], inc="dv")
                    k.dma("gp", lambda e, t=t, par=par: e.dma_start(
                        x1_s[t * 128:(t + 1) * 128, :], x1t[par][:, :]),
                        waits=[("dv", py_free)])
                    x1dma = k.cnt["gp"]
                    c_a = k.step("dv", lambda e, par=par: e.reduce_sum(
                        small[:, 0:1], x1t[par][:, :], AX.X), inc="dv")
                    c_sq = k.step("ac", lambda e, par=par: e.activation(
                        sqd[:, :], x1t[par][:, :], AF.Square, accum_out=small[:, 1:2]),
                        waits=[("dv", py_free), ("dv", prev_var4)], inc="ac")
                    c_b = k.step("dv", lambda e: e.tensor_scalar(
                        small[:, 2:3], small[:, 0:1], 1.0 / E, None, ALU.mult),
                        waits=[("dv", c_a)], inc="dv")
                    c_c = k.step("dv", lambda e: e.tensor_mul(
                        small[:, 3:4], small[:, 2:3], small[:, 2:3]),
                        waits=[("dv", c_b)], inc="dv")
                    c_var = k.step("dv", lambda e: e.tensor_scalar(
                        small[:, 4:5], small[:, 1:2], 1.0 / E, small[:, 3:4],
                        ALU.mult, ALU.subtract),
                        waits=[("ac", c_sq), ("dv", c_c)], inc="dv")
                    prev_var4 = c_var
                    c_ln = k.step("ac", lambda e: e.activation(
                        small[:, 5:6], small[:, 4:5], AF.Ln, bias=c_eln),
                        waits=[("dv", c_var)], inc="ac")
                    c_rs = k.step("ac", lambda e: e.activation(
                        small[:, 6:7], small[:, 5:6], AF.Exp, scale=-0.5),
                        waits=[("ac", c_ln)], inc="ac")
                    c_xh = k.step("dv", lambda e, par=par: e.tensor_scalar(
                        xh2[:, :], x1t[par][:, :], small[:, 2:3], small[:, 6:7],
                        ALU.subtract, ALU.mult),
                        waits=[("ac", c_rs), ("pe", xh_users), ("gp", x1dma)], inc="dv")
                    x1_free[par] = c_xh
                    for g in range(4):
                        pp = g % 2
                        for j in range(4):
                            ec = g * 4 + j
                            w = [("dv", c_xh)]
                            if j == 0 and ev[pp]:
                                w.append(("dv", ev[pp]))
                            k.step("pe", lambda e, ec=ec, pp=pp, j=j: e.transpose(
                                PT[pp][:, j * 128:(j + 1) * 128],
                                xh2[:, ec * 128:(ec + 1) * 128], identb[:, :]), waits=w)
                        c_tr = k.tag("pe")
                        if g == 3:
                            xh_users = c_tr
                        w = [("pe", c_tr)]
                        if stg_user[par]:
                            w.append(("gp", stg_user[par]))
                        dst = bass.AP(x2stg[par], g * 4 * 128,
                                      [[E, 128], [128, 4], [1, 128]])
                        ev[pp] = k.step("dv", lambda e, pp=pp, dst=dst: e.tensor_copy(
                            dst, PT[pp][:, :]), waits=w, inc="dv")
                    dstx = bass.AP(xhT_s, t * 128, [[T, 128], [128 * T, NEC], [1, 128]])
                    stg_user[par] = k.dma("gp", lambda e, par=par, dstx=dstx: e.dma_start(
                        dstx, x2stg[par][:, :].rearrange("p (ec c) -> p ec c", ec=NEC)),
                        waits=[("dv", ev[0]), ("dv", ev[1])])
                k.barrier()
                k.flush()
        ctx34.close()

        if phases >= 5:
            with contextlib.ExitStack() as p5:
                xh2T = p5.enter_context(nc.sbuf_tensor("p5xh", [128, NEC * TOWN], BF16))
                dW = 2048
                d_sh = p5.enter_context(nc.sbuf_tensor("p5sh", [128, dW], I32))
                d_m3 = p5.enter_context(nc.sbuf_tensor("p5m3", [128, dW], I32))
                d_f8 = p5.enter_context(nc.sbuf_tensor("p5f8", [128, dW], I32))
                d_m3f = p5.enter_context(nc.sbuf_tensor("p5m3f", [128, dW], F32))
                d_r1 = p5.enter_context(nc.sbuf_tensor("p5r1", [128, dW], F32))
                d_r2 = p5.enter_context(nc.sbuf_tensor("p5r2", [128, dW], F32))
                d_vv = p5.enter_context(nc.sbuf_tensor("p5vv", [128, dW], F32))
                d_tf = p5.enter_context(nc.sbuf_tensor("p5tf", [128, dW], F32))
                wrd = [p5.enter_context(nc.sbuf_tensor(f"p5wd{i}", [128, 256], I32))
                       for i in range(4)]
                scl = [p5.enter_context(nc.sbuf_tensor(f"p5sc{i}", [128, 16], F32))
                       for i in range(4)]
                wgo = [p5.enter_context(nc.sbuf_tensor(f"p5go{i}", [128, dW], BF16))
                       for i in range(2)]
                wuo = [p5.enter_context(nc.sbuf_tensor(f"p5uo{i}", [128, dW], BF16))
                       for i in range(2)]
                wgT = [p5.enter_context(nc.sbuf_tensor(f"p5gT{i}", [128, NEC * 128], BF16))
                       for i in range(2)]
                wuT = [p5.enter_context(nc.sbuf_tensor(f"p5uT{i}", [128, NEC * 128], BF16))
                       for i in range(2)]
                sg = p5.enter_context(nc.sbuf_tensor("p5sg", [128, 512], F32))
                hstg = [p5.enter_context(nc.sbuf_tensor(f"p5h{i}", [128, TOWN], BF16))
                        for i in range(2)]
                PG = [p5.enter_context(nc.psum_tensor(f"p5pg{i}", [128, 512], F32))
                      for i in range(2)]
                PU = [p5.enter_context(nc.psum_tensor(f"p5pu{i}", [128, 512], F32))
                      for i in range(2)]
                PT = [p5.enter_context(nc.psum_tensor(f"p5pt{i}", [128, 512], BF16))
                      for i in range(2)]
                dbufs = dict(d_sh=d_sh, d_m3=d_m3, d_f8=d_f8, d_m3f=d_m3f,
                             d_r1=d_r1, d_r2=d_r2, d_vv=d_vv, d_tf=d_tf,
                             d_W=dW, c_m4=c_m4, c_m6=c_m6)

                xload = k.dma("sy", lambda e: e.dma_start(
                    xh2T[:, :].rearrange("p (ec c) -> p ec c", ec=NEC),
                    bass.AP(xhT_s, 0, [[T, 128], [128 * T, NEC], [1, TOWN]])))

                def ld_words(it, slot_g, slot_u, after=()):
                    r0 = it * 128
                    k.dma("gp", lambda e, r0=r0, slot_g=slot_g: e.dma_start(
                        wrd[slot_g][:, :], gup[r0:r0 + 128, :]), waits=after)
                    k.dma("gp", lambda e, r0=r0, slot_u=slot_u: e.dma_start(
                        wrd[slot_u][:, :], gup[IFF + r0:IFF + r0 + 128, :]))
                    k.dma("gp", lambda e, r0=r0, slot_g=slot_g: e.dma_start(
                        scl[slot_g][:, :], gus[r0:r0 + 128, :]))
                    return k.dma("gp", lambda e, r0=r0, slot_u=slot_u: e.dma_start(
                        scl[slot_u][:, :], gus[IFF + r0:IFF + r0 + 128, :]))

                def dequant(words, sclf, outb, waits):
                    c0 = k.step("dv", lambda e, s=sclf: e.tensor_scalar(
                        s[:, :], s[:, :], 0.5, None, ALU.mult), waits=waits, inc="dv")
                    for j in range(8):
                        dst = bass.AP(d_sh, j, [[dW, 128], [8, 256]])
                        k.step("dv", lambda e, d=dst, s=4 * j, w=words: e.tensor_scalar(
                            d, w[:, :], s, None, ALU.logical_shift_right))
                    k.step("dv", lambda e: e.tensor_scalar(
                        d_m3[:, :], d_sh[:, :], 7, None, ALU.bitwise_and))
                    k.step("dv", lambda e: e.tensor_scalar(
                        d_f8[:, :], d_sh[:, :], 8, None, ALU.bitwise_and))
                    c1 = k.step("dv", lambda e: e.tensor_copy(d_m3f[:, :], d_m3[:, :]),
                                inc="dv")
                    k.step("ac", lambda e: e.activation(
                        d_r1[:, :], d_m3f[:, :], AF.Relu, bias=c_m4),
                        waits=[("dv", c1)])
                    a1 = k.step("ac", lambda e: e.activation(
                        d_r2[:, :], d_m3f[:, :], AF.Relu, bias=c_m6), inc="ac")
                    a2 = k.step("ac", lambda e: e.activation(
                        d_tf[:, :], d_f8[:, :], AF.Copy, bias=1.0, scale=-0.25),
                        inc="ac")
                    k.step("dv", lambda e: e.tensor_add(d_vv[:, :], d_m3f[:, :],
                                                        d_r1[:, :]),
                           waits=[("ac", a1)])
                    k.step("dv", lambda e: e.scalar_tensor_tensor(
                        d_vv[:, :], d_r2[:, :], 2.0, d_vv[:, :], ALU.mult, ALU.add))
                    scv = bass.AP(sclf, 0, [[16, 128], [1, 16], [0, 128]])
                    k.step("dv", lambda e, scv=scv: e.tensor_tensor(
                        d_tf[:, :].rearrange("p (g f) -> p g f", g=16),
                        d_tf[:, :].rearrange("p (g f) -> p g f", g=16), scv, ALU.mult),
                        waits=[("ac", a2)])
                    return k.step("dv", lambda e, o=outb: e.tensor_mul(
                        o[:, :], d_vv[:, :], d_tf[:, :]), inc="dv")

                def trans16(src, dstT, wdv, ptfree):
                    evs = []
                    for g in range(4):
                        pp = g % 2
                        for j in range(4):
                            ec = g * 4 + j
                            w = [("dv", wdv)]
                            if j == 0 and ptfree[pp]:
                                w.append(("dv", ptfree[pp]))
                            k.step("pe", lambda e, ec=ec, pp=pp, j=j, src=src:
                                   e.transpose(PT[pp][:, j * 128:(j + 1) * 128],
                                               src[:, ec * 128:(ec + 1) * 128],
                                               identb[:, :]), waits=w)
                        c_tr = k.tag("pe")
                        ptfree[pp] = k.step(
                            "dv", lambda e, pp=pp, g=g, dstT=dstT: e.tensor_copy(
                                dstT[:, g * 512:(g + 1) * 512], PT[pp][:, :]),
                            waits=[("pe", c_tr)], inc="dv")
                        evs.append(ptfree[pp])
                    return evs[-1], ptfree

                ptfree = [0, 0]
                wready = {}
                hs_user = [0, 0]
                pg_free = [0, 0]
                pu_free = [0, 0]
                sg_user = 0
                ld_done = {0: ld_words(0, 0, 1)}
                for it in range(NIT):
                    par = it % 2
                    sg_, su_ = (par * 2), (par * 2 + 1)
                    if it + 1 < NIT:
                        npar = (it + 1) % 2
                        w = [("dv", wready[it - 1])] if it >= 1 else []
                        ld_done[it + 1] = ld_words(it + 1, npar * 2, npar * 2 + 1,
                                                   after=w)
                    wg_dv = dequant(wrd[sg_], scl[sg_], wgo[par],
                                    [("gp", ld_done[it])])
                    wu_dv = dequant(wrd[su_], scl[su_], wuo[par], [])
                    _, ptfree = trans16(wgo[par], wgT[par], wg_dv, ptfree)
                    lastT, ptfree = trans16(wuo[par], wuT[par], wu_dv, ptfree)
                    wready[it] = lastT
                    for tc in range(4):
                        tp = tc % 2
                        for ec in range(NEC):
                            w = []
                            if ec == 0:
                                w = [("dv", lastT), ("sy", xload)]
                                if pg_free[tp]:
                                    w.append(("ac", pg_free[tp]))
                            k.step("pe", lambda e, ec=ec, tc=tc, par=par, tp=tp: e.matmul(
                                PG[tp][:, :],
                                wgT[par][:, ec * 128:(ec + 1) * 128],
                                xh2T[:, ec * TOWN + tc * 512:ec * TOWN + (tc + 1) * 512],
                                start=(ec == 0), stop=(ec == NEC - 1)), waits=w)
                        for ec in range(NEC):
                            w = []
                            if ec == 0 and pu_free[tp]:
                                w = [("dv", pu_free[tp])]
                            k.step("pe", lambda e, ec=ec, tc=tc, par=par, tp=tp: e.matmul(
                                PU[tp][:, :],
                                wuT[par][:, ec * 128:(ec + 1) * 128],
                                xh2T[:, ec * TOWN + tc * 512:ec * TOWN + (tc + 1) * 512],
                                start=(ec == 0), stop=(ec == NEC - 1)), waits=w)
                        mmc = k.tag("pe")
                        w = [("pe", mmc)]
                        if sg_user:
                            w.append(("dv", sg_user))
                        c_si = k.step("ac", lambda e, tp=tp: e.activation(
                            sg[:, :], PG[tp][:, :], AF.Silu), waits=w, inc="ac")
                        pg_free[tp] = c_si
                        w = [("ac", c_si)]
                        if tc == 0 and hs_user[par]:
                            w.append(("gp", hs_user[par]))
                        hm = k.step(
                            "dv", lambda e, tc=tc, par=par, tp=tp: e.tensor_mul(
                                hstg[par][:, tc * 512:(tc + 1) * 512], sg[:, :],
                                PU[tp][:, :]), waits=w, inc="dv")
                        pu_free[tp] = hm
                        sg_user = hm
                    hs_user[par] = k.dma("gp", lambda e, it=it, par=par: e.dma_start(
                        hT_s[it, :, :], hstg[par][:, :]),
                        waits=[("dv", pu_free[3 % 2])])
                k.barrier()
                k.flush()

        if phases >= 6:
            with contextlib.ExitStack() as p6:
                dW = 2816
                d_sh = p6.enter_context(nc.sbuf_tensor("p6sh", [128, dW], I32))
                d_m3 = p6.enter_context(nc.sbuf_tensor("p6m3", [128, dW], I32))
                d_f8 = p6.enter_context(nc.sbuf_tensor("p6f8", [128, dW], I32))
                d_m3f = p6.enter_context(nc.sbuf_tensor("p6m3f", [128, dW], F32))
                d_r1 = p6.enter_context(nc.sbuf_tensor("p6r1", [128, dW], F32))
                d_r2 = p6.enter_context(nc.sbuf_tensor("p6r2", [128, dW], F32))
                d_vv = p6.enter_context(nc.sbuf_tensor("p6vv", [128, dW], F32))
                d_tf = p6.enter_context(nc.sbuf_tensor("p6tf", [128, dW], F32))
                wrd = [p6.enter_context(nc.sbuf_tensor(f"p6wd{i}", [128, 704], I32))
                       for i in range(2)]
                scl = [p6.enter_context(nc.sbuf_tensor(f"p6sc{i}", [128, 44], F32))
                       for i in range(2)]
                wdo = p6.enter_context(nc.sbuf_tensor("p6do", [128, IFF], BF16))
                wdT = p6.enter_context(nc.sbuf_tensor("p6dT", [128, NIT * 512], BF16))
                hsl = [p6.enter_context(nc.sbuf_tensor(f"p6h{i}", [128, NIT * 128], BF16))
                       for i in range(2)]
                yst = [p6.enter_context(nc.sbuf_tensor(f"p6y{i}", [128, 512], F32))
                       for i in range(2)]
                ydump = p6.enter_context(nc.sbuf_tensor("p6yd", [128, 512], F32))
                PY = [p6.enter_context(nc.psum_tensor(f"p6py{i}", [128, 512], F32))
                      for i in range(2)]
                PT = [p6.enter_context(nc.psum_tensor(f"p6pt{i}", [128, 512], BF16))
                      for i in range(2)]

                def deq_half(words, sclf, outb, half, waits):
                    pw0 = half * 352
                    for j in range(8):
                        dst = bass.AP(d_sh, j, [[dW, 128], [8, 352]])
                        w = waits if j == 0 else ()
                        k.step("dv", lambda e, d=dst, s=4 * j, w2=words, pw0=pw0:
                               e.tensor_scalar(d, w2[:, pw0:pw0 + 352], s,
                                               None, ALU.logical_shift_right), waits=w)
                    k.step("dv", lambda e: e.tensor_scalar(
                        d_m3[:, :], d_sh[:, :], 7, None, ALU.bitwise_and))
                    k.step("dv", lambda e: e.tensor_scalar(
                        d_f8[:, :], d_sh[:, :], 8, None, ALU.bitwise_and))
                    c1 = k.step("dv", lambda e: e.tensor_copy(d_m3f[:, :], d_m3[:, :]),
                                inc="dv")
                    k.step("ac", lambda e: e.activation(
                        d_r1[:, :], d_m3f[:, :], AF.Relu, bias=c_m4), waits=[("dv", c1)])
                    a1 = k.step("ac", lambda e: e.activation(
                        d_r2[:, :], d_m3f[:, :], AF.Relu, bias=c_m6), inc="ac")
                    a2 = k.step("ac", lambda e: e.activation(
                        d_tf[:, :], d_f8[:, :], AF.Copy, bias=1.0, scale=-0.25),
                        inc="ac")
                    k.step("dv", lambda e: e.tensor_add(
                        d_vv[:, :], d_m3f[:, :], d_r1[:, :]), waits=[("ac", a1)])
                    k.step("dv", lambda e: e.scalar_tensor_tensor(
                        d_vv[:, :], d_r2[:, :], 2.0, d_vv[:, :], ALU.mult, ALU.add))
                    scv = bass.AP(sclf, half * 22, [[44, 128], [1, 22], [0, 128]])
                    k.step("dv", lambda e, scv=scv: e.tensor_tensor(
                        d_tf[:, :].rearrange("p (g f) -> p g f", g=22),
                        d_tf[:, :].rearrange("p (g f) -> p g f", g=22), scv, ALU.mult),
                        waits=[("ac", a2)])
                    return k.step("dv", lambda e, o=outb, half=half: e.tensor_mul(
                        o[:, half * dW:(half + 1) * dW], d_vv[:, :], d_tf[:, :]),
                        inc="dv")

                hld = {}
                py_free = [0, 0]
                yst_user = [(0, 0), (0, 0)]
                ysq_user = [0, 0]
                for ec in range(4):
                    trlast = 0
                    ptfree = [0, 0]
                    for et in range(4):
                        par = et % 2
                        r0 = ec * 512 + et * 128
                        w = [("dv", trlast)] if trlast else []
                        c_l = k.dma("gp", lambda e, r0=r0, par=par: e.dma_start(
                            wrd[par][:, :], dnp[r0:r0 + 128, :]), waits=w)
                        k.dma("gp", lambda e, r0=r0, par=par: e.dma_start(
                            scl[par][:, :], dns[r0:r0 + 128, :]))
                        c_l = k.cnt["gp"]
                        c_h = k.step("dv", lambda e, par=par: e.tensor_scalar(
                            scl[par][:, :], scl[par][:, :], 0.5, None, ALU.mult),
                            waits=[("gp", c_l)], inc="dv")
                        d1 = deq_half(wrd[par], scl[par], wdo, 0, [("dv", c_h)])
                        d2 = deq_half(wrd[par], scl[par], wdo, 1, [])
                        for gi in range(11):
                            pp = gi % 2
                            for j in range(4):
                                ic = gi * 4 + j
                                w = [("dv", d2)]
                                if j == 0 and ptfree[pp]:
                                    w.append(("dv", ptfree[pp]))
                                k.step("pe", lambda e, ic=ic, pp=pp, j=j: e.transpose(
                                    PT[pp][:, j * 128:(j + 1) * 128],
                                    wdo[:, ic * 128:(ic + 1) * 128], identb[:, :]),
                                    waits=w)
                            c_tr = k.tag("pe")
                            dst = bass.AP(wdT, (gi * 4) * 512 + et * 128,
                                          [[NIT * 512, 128], [512, 4], [1, 128]])
                            ptfree[pp] = k.step(
                                "dv", lambda e, pp=pp, dst=dst: e.tensor_copy(
                                    dst, PT[pp][:, :]), waits=[("pe", c_tr)], inc="dv")
                        trlast = ptfree[(10) % 2]
                    wd_ready = trlast
                    if ec == 0:
                        hld[0] = k.dma("sy", lambda e: e.dma_start(
                            hsl[0][:, :].rearrange("p (ic c) -> p ic c", ic=NIT),
                            bass.AP(hT_s, 0, [[TOWN, 128], [128 * TOWN, NIT],
                                              [1, 128]])))
                    for t in range(NTO):
                        par = t % 2
                        if t + 1 < NTO or ec < 3:
                            nt_ = t + 1 if t + 1 < NTO else 0
                            npar = (t + 1) % 2
                            off = nt_ * 128
                            w = [("pe", k.cnt["pe"])] if True else []
                            hld[nt_] = k.dma(
                                "sy", lambda e, off=off, npar=npar: e.dma_start(
                                    hsl[npar][:, :].rearrange(
                                        "p (ic c) -> p ic c", ic=NIT),
                                    bass.AP(hT_s, off,
                                            [[TOWN, 128], [128 * TOWN, NIT],
                                             [1, 128]])), waits=w)
                        for ic in range(NIT):
                            w = []
                            if ic == 0:
                                w = [("dv", wd_ready), ("sy", hld[t])]
                                if py_free[par]:
                                    w.append(("dv", py_free[par]))
                            k.step("pe", lambda e, ic=ic, t=t, par=par: e.matmul(
                                PY[par][:, :],
                                hsl[par][:, ic * 128:(ic + 1) * 128],
                                wdT[:, ic * 512:(ic + 1) * 512],
                                start=(ic == 0), stop=(ic == NIT - 1)), waits=w)
                        mmc = k.tag("pe")
                        w = [("pe", mmc)]
                        if yst_user[par][0]:
                            w.append(("gp", yst_user[par][0]))
                        if yst_user[par][1]:
                            w.append(("ac", yst_user[par][1]))
                        py_free[par] = k.step("dv", lambda e, par=par: e.tensor_copy(
                            yst[par][:, :], PY[par][:, :]), waits=w, inc="dv")
                        ydma = k.dma("gp", lambda e, t=t, ec=ec, par=par: e.dma_start(
                            y_s[t * 128:(t + 1) * 128, ec * 512:(ec + 1) * 512],
                            yst[par][:, :]), waits=[("dv", py_free[par])])
                        w = [("dv", py_free[par])]
                        if ysq_user[par]:
                            w.append(("dv", ysq_user[par]))
                        c_sq = k.step("ac", lambda e, par=par: e.activation(
                            ydump[:, :], yst[par][:, :], AF.Square,
                            accum_out=small[:, 8 + par:9 + par]), waits=w, inc="ac")
                        yst_user[par] = (ydma, c_sq)
                        if ec == 0:
                            ysq_user[par] = k.step(
                                "dv", lambda e, t=t, par=par: e.tensor_copy(
                                    ysq[:, t:t + 1], small[:, 8 + par:9 + par]),
                                waits=[("ac", c_sq)], inc="dv")
                        else:
                            ysq_user[par] = k.step(
                                "dv", lambda e, t=t, par=par: e.tensor_add(
                                    ysq[:, t:t + 1], ysq[:, t:t + 1],
                                    small[:, 8 + par:9 + par]),
                                waits=[("ac", c_sq)], inc="dv")
                k.barrier()
                k.flush()

        if phases >= 7:
            with contextlib.ExitStack() as p7:
                yt = [p7.enter_context(nc.sbuf_tensor(f"p7y{i}", [128, E], F32))
                      for i in range(2)]
                x1t = [p7.enter_context(nc.sbuf_tensor(f"p7x{i}", [128, E], F32))
                       for i in range(2)]
                ot = [p7.enter_context(nc.sbuf_tensor(f"p7o{i}", [128, E], F32))
                      for i in range(2)]
                ot_user = [0, 0]
                rr_user = [0, 0]
                for t in range(NTO):
                    par = t % 2
                    w = [("dv", ot_user[par])] if ot_user[par] else []
                    k.dma("sy", lambda e, t=t, par=par: e.dma_start(
                        yt[par][:, :], y_s[t * 128:(t + 1) * 128, :]), waits=w)
                    c_l = k.dma("sy", lambda e, t=t, par=par: e.dma_start(
                        x1t[par][:, :], x1_s[t * 128:(t + 1) * 128, :]))
                    w = [("dv", rr_user[par])] if rr_user[par] else []
                    c_ln = k.step("ac", lambda e, t=t, par=par: e.activation(
                        small[:, 2 + 2 * par:3 + 2 * par], ysq[:, t:t + 1], AF.Ln,
                        bias=c_erms, scale=1.0 / E), waits=w, inc="ac")
                    c_r = k.step("ac", lambda e, par=par: e.activation(
                        small[:, 3 + 2 * par:4 + 2 * par],
                        small[:, 2 + 2 * par:3 + 2 * par], AF.Exp, scale=-0.5),
                        waits=[("ac", c_ln)], inc="ac")
                    c_m = k.step("dv", lambda e, par=par: e.tensor_scalar(
                        ot[par][:, :], yt[par][:, :], small[:, 3 + 2 * par:4 + 2 * par],
                        None, ALU.mult),
                        waits=[("ac", c_r), ("sy", c_l)], inc="dv")
                    rr_user[par] = c_m
                    c_o = k.step("dv", lambda e, par=par: e.tensor_add(
                        ot[par][:, :], ot[par][:, :], x1t[par][:, :]),
                        waits=[("dv", c_m)], inc="dv")
                    ot_user[par] = c_o
                    k.dma("sy", lambda e, t=t, par=par: e.dma_start(
                        out[t * 128:(t + 1) * 128, :], ot[par][:, :]),
                        waits=[("dv", c_o)], sem="st")
                k.step("sy", lambda e: e.nop(), waits=[("st", k.cnt["st"])])
                k.flush()
        k.flush()
    return nc


_ROPE_CACHE = {}


def _rope_tables():
    if "t" not in _ROPE_CACHE:
        inv = 1.0 / 10000.0 ** (np.arange(0, 64, 2)[:32].astype(np.float64) / 64)
        ang = np.outer(np.arange(T), inv)
        c = np.cos(ang)
        s = np.sin(ang)
        scl = 128.0 ** 0.25
        idx = (np.arange(E) % 64) // 2
        _ROPE_CACHE["t"] = ((c[:, idx] * scl).astype(np.float32),
                            (s[:, idx] * scl).astype(np.float32))
    return _ROPE_CACHE["t"]


def kernel(L0_attn_x, L0_attn_wq, L0_attn_wk, L0_attn_wv, L0_attn_wout,
           L0_attn_lambda_q1, L0_attn_lambda_q2, L0_attn_lambda_k1,
           L0_attn_lambda_k2, L0_attn_subln_weight,
           L0_ffn_gate_up_packed, L0_ffn_gate_up_scales,
           L0_ffn_down_packed, L0_ffn_down_scales):
    x = np.asarray(L0_attn_x)
    ropeC, ropeS = _rope_tables()
    lvec = np.concatenate([
        np.asarray(L0_attn_lambda_q1), np.asarray(L0_attn_lambda_k1),
        np.asarray(L0_attn_lambda_q2), np.asarray(L0_attn_lambda_k2)],
        axis=0).reshape(1, 256).astype(np.float32)
    common = dict(
        wqT=np.ascontiguousarray(np.asarray(L0_attn_wq).T),
        wkT=np.ascontiguousarray(np.asarray(L0_attn_wk).T),
        wvT=np.ascontiguousarray(np.asarray(L0_attn_wv).T),
        woT=np.ascontiguousarray(np.asarray(L0_attn_wout).T),
        lvec=lvec,
        swn=np.asarray(L0_attn_subln_weight).reshape(128, 1).astype(np.float32),
        gup=np.asarray(L0_ffn_gate_up_packed),
        gus=np.ascontiguousarray(np.asarray(L0_ffn_gate_up_scales).T),
        dnp=np.asarray(L0_ffn_down_packed),
        dns=np.ascontiguousarray(np.asarray(L0_ffn_down_scales).T),
        ident=np.eye(128, dtype=ml_dtypes.bfloat16),
    )
    in_maps = []
    for c in range(8):
        b, hh = c // 2, c % 2
        if hh == 0:
            perm_x = x[b]
            rc, rs = ropeC, ropeS
        else:
            perm_x = np.concatenate([x[b][TOWN:], x[b][:TOWN]], axis=0)
            rc = np.concatenate([ropeC[TOWN:], ropeC[:TOWN]], axis=0)
            rs = np.concatenate([ropeS[TOWN:], ropeS[:TOWN]], axis=0)
        m = dict(common)
        m["xb"] = np.ascontiguousarray(perm_x)
        m["ropeC"] = np.ascontiguousarray(rc)
        m["ropeS"] = np.ascontiguousarray(rs)
        in_maps.append(m)

    nc = build(phases=_PHASES, dbg=_DBG)
    res = run_bass_kernel_spmd(nc, in_maps, core_ids=list(range(8)))
    out = np.empty((4, T, E), dtype=np.float32)
    for c in range(8):
        b, hh = c // 2, c % 2
        out[b, hh * TOWN:(hh + 1) * TOWN] = res.results[c]["out"]
    if _DBG:
        kernel.last_results = res.results
    return out


# version 08
# speedup vs baseline: 1.1432x; 1.1432x over previous
import os
import math
import contextlib
from collections import defaultdict

import numpy as np
import ml_dtypes

import concourse.bass as bass
import concourse.mybir as mybir
from concourse.bass_utils import run_bass_kernel_spmd

F32 = mybir.dt.float32
BF16 = mybir.dt.bfloat16
F16 = mybir.dt.float16
I32 = mybir.dt.int32
AF = mybir.ActivationFunctionType
ALU = mybir.AluOpType
AX = mybir.AxisListType

E = 2048
T = 4096
TOWN = 2048
NT = T // 128
NTO = TOWN // 128
NEC = E // 128
NPR = 16
IFF = 5632
NIT = IFF // 128
LAMBDA_INIT = 0.8 - 0.6 * math.exp(-0.3 * 12)
EPS_LN = 1e-5
EPS_SUB = 1e-5
EPS_RMS = 1e-6

_PHASES = int(os.environ.get("MK_PHASES", "99"))
_P3NPR = int(os.environ.get("MK_P3NPR", "16"))
_P3NQC = int(os.environ.get("MK_P3NQC", "8"))
_P3TR = int(os.environ.get("MK_P3TR", "1"))
_P3EP = int(os.environ.get("MK_P3EP", "1"))
_P3NKTP = int(os.environ.get("MK_P3NKTP", "16"))
_DBG = bool(int(os.environ.get("MK_DBG", "0")))

ENGS = ("sy", "pe", "ac", "dv", "gp")
SEM_OWNER = {"pe": "pe", "ac": "ac", "dv": "dv"}


class K:

    def __init__(self, nc):
        self.nc = nc
        self.prog = []
        self.cnt = defaultdict(int)
        self.sem = {}
        self.hw = {en: defaultdict(int) for en in ENGS}

    def step(self, eng, fn, waits=(), inc=None, incn=1):
        ws = []
        for (s, v) in waits:
            if v and v > 0:
                ws.append((s, v))
        if inc is not None:
            self.cnt[inc] += incn
        self.prog.append((eng, fn, tuple(ws), (inc, incn) if inc else None))
        return self.cnt[inc] if inc else None

    def dma(self, eng, fn, waits=(), sem=None):
        return self.step(eng, fn, waits=waits, inc=sem or eng, incn=16)

    def tag(self, inc, incn=1):
        eng, fn, ws, post = self.prog[-1]
        assert post is None, "last step already has an inc"
        self.cnt[inc] += incn
        self.prog[-1] = (eng, fn, ws, (inc, incn))
        return self.cnt[inc]

    def barrier(self):
        snap = [(s, c) for (s, c) in self.cnt.items() if c > 0]
        for eng in ENGS:
            self.step(eng, lambda e: e.nop(), waits=snap)

    def flush(self):
        if not self.prog:
            return
        prog, self.prog = self.prog, []
        by_eng = {en: [] for en in ENGS}
        for (e2, fn, ws, post) in prog:
            by_eng[e2].append((fn, ws, post))

        def mk(en):
            def run(e):
                hwe = self.hw[en]
                for (fn, ws, post) in by_eng[en]:
                    for (s, v) in ws:
                        if v > hwe[s]:
                            e.wait_ge(self.sem[s], v)
                            hwe[s] = v
                    ins = fn(e)
                    if post is not None:
                        s, n = post
                        ins.then_inc(self.sem[s], n)
            return run
        with self.nc.Block() as block:
            m = {"sy": block.sync, "pe": block.tensor, "ac": block.scalar,
                 "dv": block.vector, "gp": block.gpsimd}
            for en in ENGS:
                if by_eng[en]:
                    m[en](mk(en))


def build(phases=99, dbg=False):
    nc = bass.Bass("TRN2", target_bir_lowering=False)
    k = K(nc)

    xb = nc.dram_tensor("xb", [T, E], F32, kind="ExternalInput")
    wqT = nc.dram_tensor("wqT", [E, E], F32, kind="ExternalInput")
    wkT = nc.dram_tensor("wkT", [E, E], F32, kind="ExternalInput")
    wvT = nc.dram_tensor("wvT", [E, E], F32, kind="ExternalInput")
    woT = nc.dram_tensor("woT", [E, E], F32, kind="ExternalInput")
    ropeC = nc.dram_tensor("ropeC", [T, E], F32, kind="ExternalInput")
    ropeS = nc.dram_tensor("ropeS", [T, E], F32, kind="ExternalInput")
    lvec = nc.dram_tensor("lvec", [1, 256], F32, kind="ExternalInput")
    swn = nc.dram_tensor("swn", [128, 1], F32, kind="ExternalInput")
    gup = nc.dram_tensor("gup", [2 * IFF, 256], I32, kind="ExternalInput")
    gus = nc.dram_tensor("gus", [2 * IFF, 16], F16, kind="ExternalInput")
    dnp = nc.dram_tensor("dnp", [E, 704], I32, kind="ExternalInput")
    dns = nc.dram_tensor("dns", [E, 44], F16, kind="ExternalInput")
    ident = nc.dram_tensor("ident", [128, 128], BF16, kind="ExternalInput")
    shp = nc.dram_tensor("shp", [128, 8], I32, kind="ExternalInput")

    out = nc.dram_tensor("out", [TOWN, E], F32, kind="ExternalOutput")

    skind = "ExternalOutput" if dbg else "Internal"
    xhT_s = nc.dram_tensor("xhT_s", [NEC, 128, T], BF16, kind=skind)
    kT_s = nc.dram_tensor("kT_s", [NPR, 128, T], BF16, kind=skind)
    v_s = nc.dram_tensor("v_s", [NPR, T, 128], BF16, kind=skind)
    qT_s = nc.dram_tensor("qT_s", [NPR, 128, TOWN], BF16, kind=skind)
    x1_s = nc.dram_tensor("x1_s", [TOWN, E], F32, kind=skind)
    hT_s = nc.dram_tensor("hT_s", [NIT, 128, TOWN], BF16, kind=skind)
    y_s = nc.dram_tensor("y_s", [TOWN, E], F32, kind=skind)
    dbg_aT = nc.dram_tensor("dbg_aT", [128, NPR * TOWN], BF16, kind=skind)
    dbg_mb = nc.dram_tensor("dbg_mb", [1, 8], F32, kind=skind)
    dbg_wdT = nc.dram_tensor("dbg_wdT", [4, 128, NIT * 512], BF16, kind=skind)

    with contextlib.ExitStack() as ctx:
        for s in ("sy", "gp", "pe", "ac", "dv", "st", "w5", "w6"):
            k.sem[s] = ctx.enter_context(nc.semaphore(s))

        consts = ctx.enter_context(nc.sbuf_tensor("consts", [128, 8], F32))
        onesf = ctx.enter_context(nc.sbuf_tensor("onesf", [1, 128], F32))
        identb = ctx.enter_context(nc.sbuf_tensor("identb", [128, 128], BF16))
        wcol = ctx.enter_context(nc.sbuf_tensor("wcol", [128, 1], F32))
        nlamcol = ctx.enter_context(nc.sbuf_tensor("nlamcol", [128, 1], F32))
        mbcol = ctx.enter_context(nc.sbuf_tensor("mbcol", [128, 1], F32))
        smax = ctx.enter_context(nc.sbuf_tensor("smax", [128, 2], F32))
        smaxb = ctx.enter_context(nc.sbuf_tensor("smaxb", [128, 2], BF16))
        small = ctx.enter_context(nc.sbuf_tensor("small", [128, 16], F32))
        small2 = ctx.enter_context(nc.sbuf_tensor("small2", [128, 8], F32))
        tiny = ctx.enter_context(nc.sbuf_tensor("tiny", [1, 384], F32))
        ysq = ctx.enter_context(nc.sbuf_tensor("ysq", [128, 16], F32))
        shpb = ctx.enter_context(nc.sbuf_tensor("shpb", [128, 8], I32))

        c_m4, c_m6 = consts[:, 0:1], consts[:, 1:2]
        c_eln, c_esub, c_erms = consts[:, 2:3], consts[:, 3:4], consts[:, 4:5]
        nc.const_aps.register(F32, 1.0, consts[:, 5:6])

        k.step("gp", lambda e: e.memset(consts[:, 0:1], -4.0))
        k.step("gp", lambda e: e.memset(consts[:, 1:2], -6.0))
        k.step("gp", lambda e: e.memset(consts[:, 2:3], EPS_LN))
        k.step("gp", lambda e: e.memset(consts[:, 3:4], EPS_SUB))
        k.step("gp", lambda e: e.memset(consts[:, 4:5], EPS_RMS))
        k.step("gp", lambda e: e.memset(consts[:, 5:6], 1.0))
        k.step("gp", lambda e: e.memset(smax[:, :], 0.0))
        k.step("gp", lambda e: e.memset(onesf[:, :], 1.0), inc="gp", incn=1)
        k.dma("sy", lambda e: e.dma_start(identb[:, :], ident[:, :]))
        k.dma("sy", lambda e: e.dma_start(shpb[:, :], shp[:, :]))
        k.dma("sy", lambda e: e.dma_start(wcol[:, :], swn[:, :]))
        k.dma("sy", lambda e: e.dma_start(tiny[0:1, 0:256], lvec[:, :]))
        k.barrier()

        ctx12 = ctx.enter_context(contextlib.ExitStack())
        xhT_sb = ctx12.enter_context(nc.sbuf_tensor("xhT_sb", [128, NEC * T], BF16))
        if phases >= 1:
            with contextlib.ExitStack() as p1:
                sx = [p1.enter_context(nc.sbuf_tensor(f"p1x{i}", [128, E], F32)) for i in range(2)]
                sqd = p1.enter_context(nc.sbuf_tensor("p1sqd", [128, E], F32))
                xh = p1.enter_context(nc.sbuf_tensor("p1xh", [128, E], BF16))
                ptr = [p1.enter_context(nc.psum_tensor(f"p1pt{i}", [128, 512], BF16))
                       for i in range(2)]
                ld = [0, 0]
                ev = [0, 0]
                used = [0, 0]
                xh_users = 0
                prev_var = 0
                for t in range(NT):
                    par = t % 2
                    w = [("dv", used[par])] if used[par] else []
                    ld[par] = k.dma("sy", lambda e, t=t, par=par: e.dma_start(
                        sx[par][:, :], xb[t * 128:(t + 1) * 128, :]), waits=w)
                    c_a = k.step("dv", lambda e, par=par: e.reduce_sum(
                        small[:, 0:1], sx[par][:, :], AX.X), waits=[("sy", ld[par])],
                        inc="dv")
                    c_sq = k.step("ac", lambda e, par=par: e.activation(
                        sqd[:, :], sx[par][:, :], AF.Square, accum_out=small[:, 1:2]),
                        waits=[("sy", ld[par]), ("dv", prev_var)], inc="ac")
                    c_b = k.step("dv", lambda e: e.tensor_scalar(
                        small[:, 2:3], small[:, 0:1], 1.0 / E, None, ALU.mult),
                        waits=[("dv", c_a)], inc="dv")
                    c_c = k.step("dv", lambda e: e.tensor_mul(
                        small[:, 3:4], small[:, 2:3], small[:, 2:3]),
                        waits=[("dv", c_b)], inc="dv")
                    c_var = k.step("dv", lambda e: e.tensor_scalar(
                        small[:, 4:5], small[:, 1:2], 1.0 / E, small[:, 3:4],
                        ALU.mult, ALU.subtract),
                        waits=[("ac", c_sq), ("dv", c_c)], inc="dv")
                    prev_var = c_var
                    c_ln = k.step("ac", lambda e: e.activation(
                        small[:, 5:6], small[:, 4:5], AF.Ln, bias=c_eln),
                        waits=[("dv", c_var)], inc="ac")
                    c_rs = k.step("ac", lambda e: e.activation(
                        small[:, 6:7], small[:, 5:6], AF.Exp, scale=-0.5),
                        waits=[("ac", c_ln)], inc="ac")
                    c_xh = k.step("dv", lambda e, par=par: e.tensor_scalar(
                        xh[:, :], sx[par][:, :], small[:, 2:3], small[:, 6:7],
                        ALU.subtract, ALU.mult),
                        waits=[("ac", c_rs), ("pe", xh_users)], inc="dv")
                    used[par] = c_xh
                    for g in range(4):
                        pp = g % 2
                        for j in range(4):
                            ec = g * 4 + j
                            w = [("dv", c_xh)]
                            if ev[pp]:
                                w.append(("dv", ev[pp]))
                            k.step("pe", lambda e, ec=ec, pp=pp, j=j: e.transpose(
                                ptr[pp][:, j * 128:(j + 1) * 128],
                                xh[:, ec * 128:(ec + 1) * 128], identb[:, :]),
                                waits=w if j == 0 else [("dv", c_xh)])
                        c_tr = k.tag("pe")
                        if g == 3:
                            xh_users = c_tr
                        dst = bass.AP(xhT_sb, (g * 4) * T + t * 128,
                                      [[NEC * T, 128], [T, 4], [1, 128]])
                        ev[pp] = k.step("dv", lambda e, pp=pp, dst=dst: e.tensor_copy(
                            dst, ptr[pp][:, :]), waits=[("pe", c_tr)], inc="dv")
                if dbg:
                    k.dma("gp", lambda e: e.dma_start(
                        xhT_s[:, :, :].transpose([1, 0, 2]),
                        xhT_sb[:, :].rearrange("p (ec t) -> p ec t", ec=NEC)),
                        waits=[("dv", k.cnt["dv"])])
                k.barrier()
                k.flush()

        if phases >= 2:
            with contextlib.ExitStack() as p2:
                wbf = [p2.enter_context(nc.sbuf_tensor(f"p2w{i}", [128, NEC * 512], BF16))
                       for i in range(2)]
                tbc = [p2.enter_context(nc.sbuf_tensor(f"p2tc{i}", [128, 512], F32))
                       for i in range(2)]
                tbs = [p2.enter_context(nc.sbuf_tensor(f"p2ts{i}", [128, 512], F32))
                       for i in range(2)]
                tmp1 = p2.enter_context(nc.sbuf_tensor("p2t1", [128, 512], F32))
                tmp2 = p2.enter_context(nc.sbuf_tensor("p2t2", [128, 512], F32))
                rost = [p2.enter_context(nc.sbuf_tensor(f"p2ro{i}", [128, 512], BF16))
                        for i in range(2)]
                sqn = p2.enter_context(nc.sbuf_tensor("p2sqn", [128, 512], F32))
                stg = [p2.enter_context(nc.sbuf_tensor(f"p2st{i}", [128, 512], BF16))
                       for i in range(2)]
                pj = [p2.enter_context(nc.psum_tensor(f"p2p{i}", [128, 512], F32))
                      for i in range(2)]
                ptr = [p2.enter_context(nc.psum_tensor(f"p2pt{i}", [128, 512], BF16))
                       for i in range(2)]

                projs = [("k", wkT, True, NT, 0),
                         ("q", wqT, True, NTO, 1),
                         ("v", wvT, False, NT, None)]
                chunks = [(pi, dc) for pi in range(3) for dc in range(4)]
                wload = {}
                mmlast = {}
                for ci in range(2):
                    pi, dc = chunks[ci]
                    wT = projs[pi][1]
                    wload[ci] = k.dma("gp", lambda e, wT=wT, dc=dc, ci=ci: e.dma_start(
                        wbf[ci % 2][:, :].rearrange("p (ec d) -> p ec d", d=512),
                        wT[:, dc * 512:(dc + 1) * 512].rearrange("(ec p) d -> p ec d", p=128)))

                tb_user = [0, 0]
                pj_free = [None, None]
                ptr_free = [None, None]
                stg_free = [0, 0]
                for ci, (pi, dc) in enumerate(chunks):
                    name, wT, doro, ntt, sm_i = projs[pi]
                    wpar = ci % 2
                    for t in range(ntt):
                        par = t % 2
                        if doro:
                            w = [("dv", tb_user[par])] if tb_user[par] else []
                            k.dma("sy", lambda e, t=t, dc=dc, par=par: e.dma_start(
                                tbc[par][:, :],
                                ropeC[t * 128:(t + 1) * 128, dc * 512:(dc + 1) * 512]),
                                waits=w)
                            tbl = k.dma("sy", lambda e, t=t, dc=dc, par=par: e.dma_start(
                                tbs[par][:, :],
                                ropeS[t * 128:(t + 1) * 128, dc * 512:(dc + 1) * 512]))
                        for ec in range(NEC):
                            w = []
                            if ec == 0:
                                w = [("gp", wload[ci])]
                                if pj_free[par]:
                                    w.append(pj_free[par])
                            k.step("pe", lambda e, ec=ec, t=t, par=par, wpar=wpar,
                                   first=(ec == 0), last=(ec == NEC - 1): e.matmul(
                                       pj[par][:, :],
                                       xhT_sb[:, ec * T + t * 128: ec * T + (t + 1) * 128],
                                       wbf[wpar][:, ec * 512:(ec + 1) * 512],
                                       start=first, stop=last), waits=w)
                        mmc = k.tag("pe")
                        if t == ntt - 1:
                            mmlast[ci] = mmc
                        if doro:
                            k.step("dv", lambda e, par=par: e.tensor_mul(
                                tmp1[:, :], pj[par][:, :], tbc[par][:, :]),
                                waits=[("pe", mmc), ("sy", tbl)])
                            c2 = k.step("dv", lambda e, par=par: e.tensor_mul(
                                tmp2[:, :], pj[par][:, :], tbs[par][:, :]), inc="dv")
                            pj_free[par] = ("dv", c2)
                            tb_user[par] = c2
                            t1e = bass.AP(tmp1, 0, [[512, 128], [2, 256]])
                            t1o = bass.AP(tmp1, 1, [[512, 128], [2, 256]])
                            t2e = bass.AP(tmp2, 0, [[512, 128], [2, 256]])
                            t2o = bass.AP(tmp2, 1, [[512, 128], [2, 256]])
                            roe = bass.AP(rost[par], 0, [[512, 128], [2, 256]])
                            roo = bass.AP(rost[par], 1, [[512, 128], [2, 256]])
                            w = [ptr_free[par]] if ptr_free[par] else []
                            k.step("dv", lambda e, a=roe, b=t1e, c=t2o: e.tensor_sub(a, b, c),
                                   waits=w)
                            c_ro = k.step("dv", lambda e, a=roo, b=t2e, c=t1o:
                                          e.tensor_add(a, b, c), inc="dv")
                            k.step("dv", lambda e, par=par: e.tensor_mul(
                                sqn[:, :], rost[par][:, :], rost[par][:, :]))
                            n1 = k.step("dv", lambda e: e.tensor_reduce(
                                small2[:, 0:8],
                                sqn[:, :].rearrange("p (h d) -> p h d", h=8),
                                AX.X, ALU.add), inc="dv")
                            n2 = k.step("dv", lambda e: e.tensor_reduce(
                                small[:, 7:8], small2[:, 0:8], AX.X, ALU.max),
                                waits=[("dv", n1)], inc="dv")
                            k.step("dv", lambda e, sm_i=sm_i: e.tensor_tensor(
                                smax[:, sm_i:sm_i + 1], smax[:, sm_i:sm_i + 1],
                                small[:, 7:8], ALU.max), waits=[("dv", n2)])
                            for j in range(4):
                                w = [("dv", c_ro)]
                                if j == 0 and stg_free[par]:
                                    w.append(("gp", stg_free[par]))
                                k.step("pe", lambda e, j=j, par=par: e.transpose(
                                    ptr[par][:, j * 128:(j + 1) * 128],
                                    rost[par][:, j * 128:(j + 1) * 128], identb[:, :]),
                                    waits=w)
                            c_tr = k.tag("pe")
                            c_ev = k.step("ac", lambda e, par=par: e.copy(
                                stg[par][:, :], ptr[par][:, :]),
                                waits=[("pe", c_tr)], inc="ac")
                            ptr_free[par] = ("ac", c_ev)
                            dst_s = kT_s if name == "k" else qT_s
                            dlen = T if name == "k" else TOWN
                            dst = bass.AP(dst_s, (4 * dc) * 128 * dlen + t * 128,
                                          [[dlen, 128], [128 * dlen, 4], [1, 128]])
                            stg_free[par] = k.dma(
                                "gp", lambda e, par=par, dst=dst: e.dma_start(
                                    dst, stg[par][:, :].rearrange(
                                        "p (pr d) -> p pr d", pr=4)),
                                waits=[("ac", c_ev)])
                        else:
                            w = [("pe", mmc)]
                            if stg_free[par]:
                                w.append(("gp", stg_free[par]))
                            c_ev = k.step("ac", lambda e, par=par: e.copy(
                                stg[par][:, :], pj[par][:, :]), waits=w, inc="ac")
                            pj_free[par] = ("ac", c_ev)
                            dst = bass.AP(v_s, (4 * dc) * T * 128 + t * 128 * 128,
                                          [[128, 128], [T * 128, 4], [1, 128]])
                            stg_free[par] = k.dma(
                                "gp", lambda e, par=par, dst=dst: e.dma_start(
                                    dst, stg[par][:, :].rearrange(
                                        "p (pr d) -> p pr d", pr=4)),
                                waits=[("ac", c_ev)])
                    if ci + 2 < len(chunks):
                        pi2, dc2 = chunks[ci + 2]
                        wT2 = projs[pi2][1]
                        wload[ci + 2] = k.dma(
                            "gp", lambda e, wT2=wT2, dc2=dc2, ci=ci: e.dma_start(
                                wbf[ci % 2][:, :].rearrange("p (ec d) -> p ec d", d=512),
                                wT2[:, dc2 * 512:(dc2 + 1) * 512].rearrange(
                                    "(ec p) d -> p ec d", p=128)),
                            waits=[("pe", mmlast[ci])])

                c1 = k.step("dv", lambda e: e.tensor_copy(smaxb[:, :], smax[:, :]), inc="dv")
                k.step("pe", lambda e: e.transpose(
                    bass.AP(ptr[0], 0, [[512, 1], [1, 128]]),
                    smaxb[:, 0:1], identb[:, :]), waits=[("dv", c1)])
                k.step("pe", lambda e: e.transpose(
                    bass.AP(ptr[0], 128, [[512, 1], [1, 128]]),
                    smaxb[:, 1:2], identb[:, :]))
                c2 = k.tag("pe")
                k.step("dv", lambda e: e.tensor_reduce(
                    tiny[0:1, 256:257], bass.AP(ptr[0], 0, [[512, 1], [1, 128]]),
                    AX.X, ALU.max), waits=[("pe", c2)])
                c3 = k.step("dv", lambda e: e.tensor_reduce(
                    tiny[0:1, 257:258], bass.AP(ptr[0], 128, [[512, 1], [1, 128]]),
                    AX.X, ALU.max), inc="dv")
                c4a = k.step("ac", lambda e: e.activation(
                    tiny[0:1, 258:259], tiny[0:1, 256:257], AF.Ln),
                    waits=[("dv", c3)], inc="ac")
                c4 = k.step("ac", lambda e: e.activation(
                    tiny[0:1, 259:260], tiny[0:1, 257:258], AF.Ln),
                    waits=[("ac", c4a)], inc="ac")
                c5 = k.step("dv", lambda e: e.tensor_add(
                    tiny[0:1, 260:261], tiny[0:1, 258:259], tiny[0:1, 259:260]),
                    waits=[("ac", c4)], inc="dv")
                c6 = k.step("ac", lambda e: e.activation(
                    tiny[0:1, 261:262], tiny[0:1, 260:261], AF.Exp, scale=0.5),
                    waits=[("dv", c5)], inc="ac")
                c7 = k.step("dv", lambda e: e.tensor_scalar(
                    tiny[0:1, 262:263], tiny[0:1, 261:262], -1.0, 32.0,
                    ALU.mult, ALU.add), waits=[("ac", c6)], inc="dv")
                l1 = k.step("dv", lambda e: e.tensor_mul(
                    tiny[0:1, 264:328], tiny[0:1, 0:64], tiny[0:1, 64:128]), inc="dv")
                l2 = k.step("dv", lambda e: e.reduce_sum(
                    tiny[0:1, 328:329], tiny[0:1, 264:328], AX.X),
                    waits=[("dv", l1)], inc="dv")
                l3 = k.step("dv", lambda e: e.tensor_mul(
                    tiny[0:1, 264:328], tiny[0:1, 128:192], tiny[0:1, 192:256]),
                    waits=[("dv", l2)], inc="dv")
                c8 = k.step("dv", lambda e: e.reduce_sum(
                    tiny[0:1, 329:330], tiny[0:1, 264:328], AX.X),
                    waits=[("dv", l3)], inc="dv")
                l4 = k.step("ac", lambda e: e.activation(
                    tiny[0:1, 330:331], tiny[0:1, 328:329], AF.Exp),
                    waits=[("dv", c8)], inc="ac")
                c9 = k.step("ac", lambda e: e.activation(
                    tiny[0:1, 331:332], tiny[0:1, 329:330], AF.Exp),
                    waits=[("ac", l4)], inc="ac")
                l5 = k.step("dv", lambda e: e.tensor_sub(
                    tiny[0:1, 332:333], tiny[0:1, 330:331], tiny[0:1, 331:332]),
                    waits=[("ac", c9)], inc="dv")
                c10 = k.step("dv", lambda e: e.tensor_scalar(
                    tiny[0:1, 333:334], tiny[0:1, 332:333], -1.0,
                    LAMBDA_INIT, ALU.mult, ALU.subtract),
                    waits=[("dv", l5)], inc="dv")
                k.step("pe", lambda e: e.matmul(
                    pj[0][:, 0:1], onesf[:, :], tiny[0:1, 262:263],
                    start=True, stop=True), waits=[("dv", c7), ("dv", c10)])
                c11 = k.step("pe", lambda e: e.matmul(
                    pj[0][:, 4:5], onesf[:, :], tiny[0:1, 333:334],
                    start=True, stop=True), inc="pe")
                k.step("dv", lambda e: e.tensor_copy(mbcol[:, :], pj[0][:, 0:1]),
                       waits=[("pe", c11)])
                k.step("dv", lambda e: e.tensor_copy(nlamcol[:, :], pj[0][:, 4:5]),
                       inc="dv")
                if dbg:
                    k.dma("gp", lambda e: e.dma_start(dbg_mb[:, :], tiny[0:1, 256:264]),
                          waits=[("dv", k.cnt["dv"])])
                k.barrier()
                k.flush()
        ctx12.close()

        ctx34 = ctx.enter_context(contextlib.ExitStack())
        atT_sb = ctx34.enter_context(nc.sbuf_tensor("atT_sb", [128, NPR * TOWN], BF16))
        if phases >= 3:
            with contextlib.ExitStack() as p3:
                SAps = p3.enter_context(nc.psum_tensor("p3SA", [128, 1024], F32))
                SBps = p3.enter_context(nc.psum_tensor("p3SB", [128, 1024], F32))
                U4 = p3.enter_context(nc.psum_tensor("p3U", [128, 2048], F32))
                Ebuf = [p3.enter_context(nc.sbuf_tensor(f"p3E{i}", [128, 1024], BF16))
                        for i in range(2)]
                ksl = [p3.enter_context(nc.sbuf_tensor(f"p3k{i}", [128, T], BF16))
                       for i in range(2)]
                vsl = [p3.enter_context(nc.sbuf_tensor(f"p3v{i}", [128, 32 * 129], BF16))
                       for i in range(2)]
                qsl = [p3.enter_context(nc.sbuf_tensor(f"p3q{i}", [128, TOWN], BF16))
                       for i in range(2)]
                astg = p3.enter_context(nc.sbuf_tensor("p3as", [128, TOWN], BF16))
                ubuf = p3.enter_context(nc.sbuf_tensor("p3ub", [128, 516], F32))
                ga = p3.enter_context(nc.sbuf_tensor("p3ga", [128, 256], F32))
                gb = p3.enter_context(nc.sbuf_tensor("p3gb", [128, 256], F32))
                Sgrp = [SAps, SBps]
                SAbf = SAps[:, :].bitcast(BF16)
                SBbf = SBps[:, :].bitcast(BF16)

                for i in range(2):
                    k.step("gp", lambda e, i=i: e.memset(
                        bass.AP(vsl[i], 128, [[32 * 129, 128], [129, 32]]), 1.0),
                        inc="gp", incn=1)

                slab_ld = {}

                def load_slabs(pr, wpe):
                    par = pr % 2
                    w = [("pe", wpe)] if wpe else []
                    k.dma("sy", lambda e, pr=pr, par=par: e.dma_start(
                        ksl[par][:, :], kT_s[pr, :, :]), waits=w)
                    k.dma("sy", lambda e, pr=pr, par=par: e.dma_start(
                        qsl[par][:, :], qT_s[pr, :, :]))
                    dstv = bass.AP(vsl[par], 0, [[32 * 129, 128], [129, 32], [1, 128]])
                    slab_ld[pr] = k.dma("sy", lambda e, pr=pr, dstv=dstv: e.dma_start(
                        dstv, v_s[pr, :, :].rearrange("(kt p) d -> p kt d", p=128)))

                load_slabs(0, 0)
                NQC = _P3NQC
                UPP = 16 * NQC
                NU = _P3NPR * UPP

                def uidx(i):
                    return i // UPP, (i % UPP) // 16, i % 16

                exp_hist = {}
                av_hist = {}
                qk_hist = {}
                pr_lastav = {-1: 0}
                ep_done = 0
                ep_chain = 0
                tr_evac = 0

                def emit_qk(i):
                    pr, qc, ktp = uidx(i)
                    spar = i % 2
                    S = Sgrp[spar]
                    par = pr % 2
                    w = [("sy", slab_ld[pr])]
                    if i - 2 in exp_hist:
                        w.append(("ac", exp_hist[i - 2][1]))
                    if i % UPP == 1 and tr_evac:
                        w.append(("dv", tr_evac))
                    tags = []
                    for sub in range(2):
                        kt = 2 * ktp + sub
                        for h01 in range(2):
                            ww = w if (sub == 0 and h01 == 0) else ()
                            k.step("pe", lambda e, S=S, sub=sub, h01=h01, kt=kt,
                                   par=par, qc=qc: e.matmul(
                                       S[:, h01 * 512 + sub * 256:
                                         h01 * 512 + sub * 256 + 256],
                                       ksl[par][h01 * 64:(h01 + 1) * 64,
                                                kt * 128:(kt + 1) * 128],
                                       qsl[par][h01 * 64:(h01 + 1) * 64,
                                                qc * 256:(qc + 1) * 256],
                                       start=True, stop=True), waits=ww)
                        tags.append(k.tag("pe"))
                    qk_hist[i] = tuple(tags)

                emit_qk(0)
                for i in range(NU):
                    pr, qc, ktp = uidx(i)
                    par = pr % 2
                    spar = i % 2
                    S = Sgrp[spar]
                    epar = i % 2
                    w = [("pe", qk_hist[i][1])]
                    if i - 2 in av_hist:
                        w.append(("pe", av_hist[i - 2]))
                    ec_ = k.step(
                        "ac", lambda e, S=S, epar=epar: e.activation(
                            Ebuf[epar][:, :], S[:, :], AF.Exp,
                            bias=mbcol[:, 0:1]), waits=w, inc="ac")
                    exp_hist[i] = (ec_, ec_)
                    if i % UPP == 16 * (NQC // 2) and pr + 1 < _P3NPR:
                        load_slabs(pr + 1, pr_lastav[pr - 1])
                    if i + 1 < NU:
                        emit_qk(i + 1)
                    for sub in range(2):
                        kt = 2 * ktp + sub
                        w = [("ac", exp_hist[i][sub])]
                        if sub == 0 and ktp == 0 and ep_done:
                            w.append(("dv", ep_done))
                        first_mm = True
                        for h01 in range(2):
                            for qs in range(2):
                                ww = w if first_mm else ()
                                first_mm = False
                                ucol = (h01 * 2 + qs) * 512
                                eoff = h01 * 512 + sub * 256 + qs * 128
                                k.step("pe", lambda e, epar=epar, kt=kt,
                                       par=par, ucol=ucol, eoff=eoff: e.matmul(
                                           U4[:, ucol:ucol + 129],
                                           Ebuf[epar][:, eoff:eoff + 128],
                                           vsl[par][:, kt * 129:kt * 129 + 129],
                                           start=(kt == 0), stop=(kt == 31)),
                                       waits=ww)
                    av_hist[i] = k.tag("pe")
                    if ktp != 15:
                        continue
                    avl = av_hist[i]
                    if _P3EP:
                        usrc = bass.AP(U4, 0, [[2048, 128], [512, 4], [1, 129]])
                        ep_done = k.step("dv", lambda e, usrc=usrc: e.tensor_copy(
                            ubuf[:, :].rearrange("p (j c) -> p j c", j=4), usrc),
                            waits=[("pe", avl)], inc="dv")
                        ssrc = bass.AP(ubuf, 128, [[516, 128], [129, 4]])
                        h1 = k.step("dv", lambda e, ssrc=ssrc: e.reciprocal(
                            small[:, 8:12], ssrc), waits=[("dv", ep_done)], inc="dv")
                        nlv = bass.AP(nlamcol, 0, [[1, 128], [0, 2]])
                        h2 = k.step("dv", lambda e, nlv=nlv: e.tensor_tensor(
                            small[:, 12:14], small[:, 10:12], nlv, ALU.mult),
                            waits=[("dv", h1)], inc="dv")
                        u0v = bass.AP(ubuf, 0, [[516, 128], [129, 2], [1, 128]])
                        u1v = bass.AP(ubuf, 2 * 129, [[516, 128], [129, 2], [1, 128]])
                        r0bc = bass.AP(small, 8, [[16, 128], [1, 2], [0, 128]])
                        nlbc = bass.AP(small, 12, [[16, 128], [1, 2], [0, 128]])
                        h3 = k.step("dv", lambda e, u0v=u0v, r0bc=r0bc: e.tensor_tensor(
                            ga[:, :].rearrange("p (j c) -> p j c", j=2), u0v, r0bc,
                            ALU.mult), waits=[("dv", h2)], inc="dv")
                        h4 = k.step("dv", lambda e, u1v=u1v, nlbc=nlbc: e.tensor_tensor(
                            gb[:, :].rearrange("p (j c) -> p j c", j=2), u1v, nlbc,
                            ALU.mult), waits=[("dv", h3)], inc="dv")
                        h5 = k.step("dv", lambda e: e.tensor_add(
                            ga[:, :], ga[:, :], gb[:, :]), waits=[("dv", h4)], inc="dv")
                        h6 = k.step("dv", lambda e: e.tensor_mul(
                            gb[:, :], ga[:, :], ga[:, :]), waits=[("dv", h5)], inc="dv")
                        h7 = k.step("dv", lambda e: e.tensor_reduce(
                            small[:, 14:16], gb[:, :].rearrange("p (j c) -> p j c", j=2),
                            AX.X, ALU.add), waits=[("dv", h6)], inc="dv")
                        h8 = k.step("ac", lambda e: e.activation(
                            small[:, 4:6], small[:, 14:16], AF.Ln,
                            bias=c_esub, scale=1.0 / 128), waits=[("dv", h7)], inc="ac")
                        h9 = k.step("ac", lambda e: e.activation(
                            small[:, 6:8], small[:, 4:6], AF.Exp, scale=-0.5),
                            waits=[("ac", h8)], inc="ac")
                        rsbc = bass.AP(small, 6, [[16, 128], [1, 2], [0, 128]])
                        ep_chain = k.step(
                            "dv", lambda e, qc=qc, rsbc=rsbc: e.tensor_tensor(
                                astg[:, qc * 256:(qc + 1) * 256].rearrange(
                                    "p (j c) -> p j c", j=2),
                                ga[:, :].rearrange("p (j c) -> p j c", j=2),
                                rsbc, ALU.mult), waits=[("ac", h9)], inc="dv")
                    if qc != NQC - 1:
                        continue
                    pr_lastav[pr] = av_hist[i]
                    lastexp = exp_hist[i][1]
                    for j in range(NTO if _P3TR else 0):
                        w = [("dv", ep_chain)] if j > 0 else [("dv", ep_chain),
                                                              ("ac", lastexp)]
                        k.step("pe", lambda e, j=j: e.transpose(
                            SBbf[:, j * 128:(j + 1) * 128],
                            astg[:, j * 128:(j + 1) * 128], identb[:, :]), waits=w)
                    if _P3TR:
                        c_tr = k.tag("pe")
                        tr_evac = k.step("dv", lambda e, pr=pr: e.tensor_copy(
                            atT_sb[:, pr * TOWN:(pr + 1) * TOWN], SBbf[:, 0:TOWN]),
                            waits=[("pe", c_tr)], inc="dv")
                if dbg:
                    k.dma("gp", lambda e: e.dma_start(dbg_aT[:, :], atT_sb[:, :]),
                          waits=[("dv", tr_evac)])
                k.barrier()
                k.flush()

        if phases >= 4:
            with contextlib.ExitStack() as p4:
                wouts = p4.enter_context(nc.sbuf_tensor("p4w", [128, NEC * E], BF16))
                wtmp = [p4.enter_context(nc.sbuf_tensor(f"p4wt{i}", [128, E], F32))
                        for i in range(2)]
                xt = [p4.enter_context(nc.sbuf_tensor(f"p4x{i}", [128, E], F32))
                      for i in range(2)]
                x1t = [p4.enter_context(nc.sbuf_tensor(f"p4x1{i}", [128, E], F32))
                       for i in range(2)]
                sqd = p4.enter_context(nc.sbuf_tensor("p4sqd", [128, E], F32))
                xh2 = p4.enter_context(nc.sbuf_tensor("p4xh2", [128, E], BF16))
                x2stg = [p4.enter_context(nc.sbuf_tensor(f"p4xs{i}", [128, E], BF16))
                         for i in range(2)]
                PY = p4.enter_context(nc.psum_tensor("p4py", [128, 2048], F32))
                PT = [p4.enter_context(nc.psum_tensor(f"p4pt{i}", [128, 512], BF16))
                      for i in range(2)]
                wl = {}
                for dc in range(NEC):
                    par = dc % 2
                    w = [("ac", wl[dc - 2])] if dc >= 2 else []
                    c_l = k.dma("sy", lambda e, dc=dc, par=par: e.dma_start(
                        wtmp[par][:, :], woT[dc * 128:(dc + 1) * 128, :]), waits=w)
                    wl[dc] = k.step("ac", lambda e, dc=dc, par=par: e.activation(
                        wouts[:, dc * E:(dc + 1) * E], wtmp[par][:, :], AF.Copy,
                        scale=wcol[:, 0:1]), waits=[("sy", c_l)], inc="ac")
                wready = wl[NEC - 1]
                py_free = 0
                x1_free = [0, 0]
                xh_users = 0
                prev_var4 = 0
                ev = [0, 0]
                stg_user = [0, 0]
                for t in range(NTO):
                    par = t % 2
                    w = [("dv", x1_free[par])] if x1_free[par] else []
                    c_x = k.dma("sy", lambda e, t=t, par=par: e.dma_start(
                        xt[par][:, :], xb[t * 128:(t + 1) * 128, :]), waits=w)
                    for ec in range(4):
                        for dv_ in range(NEC):
                            w = []
                            if ec == 0 and dv_ == 0:
                                w = [("ac", wready)]
                                if py_free:
                                    w.append(("dv", py_free))
                            k.step("pe", lambda e, ec=ec, dv_=dv_, t=t: e.matmul(
                                PY[:, ec * 512:(ec + 1) * 512],
                                atT_sb[:, dv_ * TOWN + t * 128:dv_ * TOWN + (t + 1) * 128],
                                wouts[:, dv_ * E + ec * 512:dv_ * E + (ec + 1) * 512],
                                start=(dv_ == 0), stop=(dv_ == NEC - 1)), waits=w)
                    mmc = k.tag("pe")
                    py_free = k.step("dv", lambda e, par=par: e.tensor_add(
                        x1t[par][:, :], PY[:, :], xt[par][:, :]),
                        waits=[("pe", mmc), ("sy", c_x)], inc="dv")
                    k.dma("gp", lambda e, t=t, par=par: e.dma_start(
                        x1_s[t * 128:(t + 1) * 128, :], x1t[par][:, :]),
                        waits=[("dv", py_free)])
                    x1dma = k.cnt["gp"]
                    c_a = k.step("dv", lambda e, par=par: e.reduce_sum(
                        small[:, 0:1], x1t[par][:, :], AX.X), inc="dv")
                    c_sq = k.step("ac", lambda e, par=par: e.activation(
                        sqd[:, :], x1t[par][:, :], AF.Square, accum_out=small[:, 1:2]),
                        waits=[("dv", py_free), ("dv", prev_var4)], inc="ac")
                    c_b = k.step("dv", lambda e: e.tensor_scalar(
                        small[:, 2:3], small[:, 0:1], 1.0 / E, None, ALU.mult),
                        waits=[("dv", c_a)], inc="dv")
                    c_c = k.step("dv", lambda e: e.tensor_mul(
                        small[:, 3:4], small[:, 2:3], small[:, 2:3]),
                        waits=[("dv", c_b)], inc="dv")
                    c_var = k.step("dv", lambda e: e.tensor_scalar(
                        small[:, 4:5], small[:, 1:2], 1.0 / E, small[:, 3:4],
                        ALU.mult, ALU.subtract),
                        waits=[("ac", c_sq), ("dv", c_c)], inc="dv")
                    prev_var4 = c_var
                    c_ln = k.step("ac", lambda e: e.activation(
                        small[:, 5:6], small[:, 4:5], AF.Ln, bias=c_eln),
                        waits=[("dv", c_var)], inc="ac")
                    c_rs = k.step("ac", lambda e: e.activation(
                        small[:, 6:7], small[:, 5:6], AF.Exp, scale=-0.5),
                        waits=[("ac", c_ln)], inc="ac")
                    c_xh = k.step("dv", lambda e, par=par: e.tensor_scalar(
                        xh2[:, :], x1t[par][:, :], small[:, 2:3], small[:, 6:7],
                        ALU.subtract, ALU.mult),
                        waits=[("ac", c_rs), ("pe", xh_users), ("gp", x1dma)], inc="dv")
                    x1_free[par] = c_xh
                    for g in range(4):
                        pp = g % 2
                        for j in range(4):
                            ec = g * 4 + j
                            w = [("dv", c_xh)]
                            if j == 0 and ev[pp]:
                                w.append(("dv", ev[pp]))
                            k.step("pe", lambda e, ec=ec, pp=pp, j=j: e.transpose(
                                PT[pp][:, j * 128:(j + 1) * 128],
                                xh2[:, ec * 128:(ec + 1) * 128], identb[:, :]), waits=w)
                        c_tr = k.tag("pe")
                        if g == 3:
                            xh_users = c_tr
                        w = [("pe", c_tr)]
                        if stg_user[par]:
                            w.append(("gp", stg_user[par]))
                        dst = bass.AP(x2stg[par], g * 4 * 128,
                                      [[E, 128], [128, 4], [1, 128]])
                        ev[pp] = k.step("dv", lambda e, pp=pp, dst=dst: e.tensor_copy(
                            dst, PT[pp][:, :]), waits=w, inc="dv")
                    dstx = bass.AP(xhT_s, t * 128, [[T, 128], [128 * T, NEC], [1, 128]])
                    stg_user[par] = k.dma("gp", lambda e, par=par, dstx=dstx: e.dma_start(
                        dstx, x2stg[par][:, :].rearrange("p (ec c) -> p ec c", ec=NEC)),
                        waits=[("dv", ev[0]), ("dv", ev[1])])
                k.barrier()
                k.flush()
        ctx34.close()

        if phases >= 5:
            with contextlib.ExitStack() as p5:
                xh2T = p5.enter_context(nc.sbuf_tensor("p5xh", [128, NEC * TOWN], BF16))
                dW = 2048
                d_sh = p5.enter_context(nc.sbuf_tensor("p5sh", [128, dW], I32))
                d_m3 = p5.enter_context(nc.sbuf_tensor("p5m3", [128, dW], I32))
                d_f8 = p5.enter_context(nc.sbuf_tensor("p5f8", [128, dW], I32))
                d_m3f = p5.enter_context(nc.sbuf_tensor("p5m3f", [128, dW], F32))
                d_r1 = p5.enter_context(nc.sbuf_tensor("p5r1", [128, dW], F32))
                d_r2 = p5.enter_context(nc.sbuf_tensor("p5r2", [128, dW], F32))
                d_vv = p5.enter_context(nc.sbuf_tensor("p5vv", [128, dW], F32))
                d_tf = p5.enter_context(nc.sbuf_tensor("p5tf", [128, dW], F32))
                wrd = [p5.enter_context(nc.sbuf_tensor(f"p5wd{i}", [128, 256], I32))
                       for i in range(4)]
                scl = [p5.enter_context(nc.sbuf_tensor(f"p5sc{i}", [128, 16], F32))
                       for i in range(4)]
                wgo = [p5.enter_context(nc.sbuf_tensor(f"p5go{i}", [128, dW], BF16))
                       for i in range(2)]
                wuo = [p5.enter_context(nc.sbuf_tensor(f"p5uo{i}", [128, dW], BF16))
                       for i in range(2)]
                wgT = [p5.enter_context(nc.sbuf_tensor(f"p5gT{i}", [128, NEC * 128], BF16))
                       for i in range(2)]
                wuT = [p5.enter_context(nc.sbuf_tensor(f"p5uT{i}", [128, NEC * 128], BF16))
                       for i in range(2)]
                sg = p5.enter_context(nc.sbuf_tensor("p5sg", [128, 512], F32))
                hstg = [p5.enter_context(nc.sbuf_tensor(f"p5h{i}", [128, TOWN], BF16))
                        for i in range(2)]
                PG = [p5.enter_context(nc.psum_tensor(f"p5pg{i}", [128, 512], F32))
                      for i in range(2)]
                PU = [p5.enter_context(nc.psum_tensor(f"p5pu{i}", [128, 512], F32))
                      for i in range(2)]
                PT = [p5.enter_context(nc.psum_tensor(f"p5pt{i}", [128, 512], BF16))
                      for i in range(2)]
                dbufs = dict(d_sh=d_sh, d_m3=d_m3, d_f8=d_f8, d_m3f=d_m3f,
                             d_r1=d_r1, d_r2=d_r2, d_vv=d_vv, d_tf=d_tf,
                             d_W=dW, c_m4=c_m4, c_m6=c_m6)

                xload = k.dma("sy", lambda e: e.dma_start(
                    xh2T[:, :].rearrange("p (ec c) -> p ec c", ec=NEC),
                    bass.AP(xhT_s, 0, [[T, 128], [128 * T, NEC], [1, TOWN]])))

                def ld_words(it, slot_g, slot_u, after=()):
                    r0 = it * 128
                    w = list(after) + [("w5", k.cnt["w5"])]
                    k.dma("gp", lambda e, r0=r0, slot_g=slot_g: e.dma_start(
                        wrd[slot_g][:, :], gup[r0:r0 + 128, :]), waits=w, sem="w5")
                    k.dma("gp", lambda e, r0=r0, slot_u=slot_u: e.dma_start(
                        wrd[slot_u][:, :], gup[IFF + r0:IFF + r0 + 128, :]),
                        sem="w5")
                    k.dma("gp", lambda e, r0=r0, slot_g=slot_g: e.dma_start(
                        scl[slot_g][:, :], gus[r0:r0 + 128, :]), sem="w5")
                    return k.dma("gp", lambda e, r0=r0, slot_u=slot_u: e.dma_start(
                        scl[slot_u][:, :], gus[IFF + r0:IFF + r0 + 128, :]),
                        sem="w5")

                def deq_front(words, p):
                    c = 1024 * p
                    wc = 128 * p
                    dst = d_sh[:, c:c + 1024].rearrange("p (w j) -> p w j", j=8)
                    src = bass.AP(words, wc, [[256, 128], [1, 128], [0, 8]])
                    shv = bass.AP(shpb, 0, [[8, 128], [0, 128], [1, 8]])
                    k.step("dv", lambda e, d=dst, src=src, shv=shv:
                           e.tensor_tensor(d, src, shv, ALU.logical_shift_right))
                    k.step("dv", lambda e, c=c: e.tensor_scalar(
                        d_m3[:, c:c + 1024], d_sh[:, c:c + 1024], 7, None,
                        ALU.bitwise_and))
                    k.step("dv", lambda e, c=c: e.tensor_scalar(
                        d_f8[:, c:c + 1024], d_sh[:, c:c + 1024], 8, None,
                        ALU.bitwise_and))
                    c1 = k.step("dv", lambda e, c=c: e.tensor_copy(
                        d_m3f[:, c:c + 1024], d_m3[:, c:c + 1024]), inc="dv")
                    k.step("ac", lambda e, c=c: e.activation(
                        d_r1[:, c:c + 1024], d_m3f[:, c:c + 1024], AF.Relu,
                        bias=c_m4), waits=[("dv", c1)])
                    a1 = k.step("ac", lambda e, c=c: e.activation(
                        d_r2[:, c:c + 1024], d_m3f[:, c:c + 1024], AF.Relu,
                        bias=c_m6), inc="ac")
                    a2 = k.step("ac", lambda e, c=c: e.activation(
                        d_tf[:, c:c + 1024], d_f8[:, c:c + 1024], AF.Copy,
                        bias=1.0, scale=-0.25), inc="ac")
                    return a1, a2

                def deq_back(sclf, outb, p, acts, pw=()):
                    a1, a2 = acts
                    c = 1024 * p
                    k.step("dv", lambda e, c=c: e.tensor_add(
                        d_vv[:, c:c + 1024], d_m3f[:, c:c + 1024],
                        d_r1[:, c:c + 1024]), waits=[("ac", a1)] + list(pw))
                    k.step("dv", lambda e, c=c: e.scalar_tensor_tensor(
                        d_vv[:, c:c + 1024], d_r2[:, c:c + 1024], 2.0,
                        d_vv[:, c:c + 1024], ALU.mult, ALU.add))
                    scv = bass.AP(sclf, 8 * p, [[16, 128], [1, 8], [0, 128]])
                    k.step("dv", lambda e, scv=scv, c=c: e.tensor_tensor(
                        d_tf[:, c:c + 1024].rearrange("p (g f) -> p g f", g=8),
                        d_tf[:, c:c + 1024].rearrange("p (g f) -> p g f", g=8),
                        scv, ALU.mult), waits=[("ac", a2)])
                    return k.step("dv", lambda e, o=outb, c=c: e.tensor_mul(
                        o[:, c:c + 1024], d_vv[:, c:c + 1024],
                        d_tf[:, c:c + 1024]), inc="dv")

                def dequant2(words_g, scl_g, out_g, words_u, scl_u, out_u, waits,
                             pe_gate=0):
                    k.step("dv", lambda e, s=scl_g: e.tensor_scalar(
                        s[:, :], s[:, :], 0.5, None, ALU.mult), waits=waits)
                    k.step("dv", lambda e, s=scl_u: e.tensor_scalar(
                        s[:, :], s[:, :], 0.5, None, ALU.mult))
                    pw = [("pe", pe_gate)] if pe_gate else []
                    ag = [deq_front(words_g, p) for p in range(2)]
                    bg = [deq_back(scl_g, out_g, p, ag[p], pw) for p in range(2)]
                    au = [deq_front(words_u, p) for p in range(2)]
                    bu = [deq_back(scl_u, out_u, p, au[p], pw) for p in range(2)]
                    return bg, bu

                def trans16(src, dstT, wdvs, ptfree):
                    evs = []
                    for g in range(4):
                        pp = g % 2
                        for j in range(4):
                            ec = g * 4 + j
                            w = [("dv", wdvs[g])]
                            if j == 0 and ptfree[pp]:
                                w.append(("ac", ptfree[pp]))
                            k.step("pe", lambda e, ec=ec, pp=pp, j=j, src=src:
                                   e.transpose(PT[pp][:, j * 128:(j + 1) * 128],
                                               src[:, ec * 128:(ec + 1) * 128],
                                               identb[:, :]), waits=w)
                        c_tr = k.tag("pe")
                        ptfree[pp] = k.step(
                            "ac", lambda e, pp=pp, g=g, dstT=dstT: e.copy(
                                dstT[:, g * 512:(g + 1) * 512], PT[pp][:, :]),
                            waits=[("pe", c_tr)], inc="ac")
                        evs.append(ptfree[pp])
                    return evs[-1], ptfree

                ptfree = [0, 0]
                wready = {}
                hs_user = [0, 0]
                pg_free = [0, 0]
                pu_free = [0, 0]
                sg_user = 0
                tr_pe = {0: 0}
                ld_done = {0: ld_words(0, 0, 1)}
                deq = {0: dequant2(wrd[0], scl[0], wgo[0], wrd[1], scl[1], wuo[0],
                                   [("w5", ld_done[0])])}
                for it in range(NIT):
                    par = it % 2
                    if it + 1 < NIT:
                        npar = (it + 1) % 2
                        w = [("ac", wready[it - 1])] if it >= 1 else []
                        ld_done[it + 1] = ld_words(it + 1, npar * 2, npar * 2 + 1,
                                                   after=w)
                    bg, bu = deq[it]
                    _, ptfree = trans16(wgo[par], wgT[par],
                                        [bg[0], bg[0], bg[1], bg[1]], ptfree)
                    lastT, ptfree = trans16(wuo[par], wuT[par],
                                            [bu[0], bu[0], bu[1], bu[1]], ptfree)
                    wready[it] = lastT
                    tr_pe[it] = k.cnt["pe"]
                    if it + 1 < NIT:
                        npar = (it + 1) % 2
                        deq[it + 1] = dequant2(
                            wrd[npar * 2], scl[npar * 2], wgo[npar],
                            wrd[npar * 2 + 1], scl[npar * 2 + 1], wuo[npar],
                            [("w5", ld_done[it + 1])],
                            pe_gate=tr_pe[it - 1] if it >= 1 else 0)
                    for tc in range(4):
                        tp = tc % 2
                        for ec in range(NEC):
                            w = []
                            if ec == 0:
                                w = [("ac", lastT), ("sy", xload)]
                                if pg_free[tp]:
                                    w.append(("ac", pg_free[tp]))
                            k.step("pe", lambda e, ec=ec, tc=tc, par=par, tp=tp: e.matmul(
                                PG[tp][:, :],
                                wgT[par][:, ec * 128:(ec + 1) * 128],
                                xh2T[:, ec * TOWN + tc * 512:ec * TOWN + (tc + 1) * 512],
                                start=(ec == 0), stop=(ec == NEC - 1)), waits=w)
                        for ec in range(NEC):
                            w = []
                            if ec == 0 and pu_free[tp]:
                                w = [("dv", pu_free[tp])]
                            k.step("pe", lambda e, ec=ec, tc=tc, par=par, tp=tp: e.matmul(
                                PU[tp][:, :],
                                wuT[par][:, ec * 128:(ec + 1) * 128],
                                xh2T[:, ec * TOWN + tc * 512:ec * TOWN + (tc + 1) * 512],
                                start=(ec == 0), stop=(ec == NEC - 1)), waits=w)
                        mmc = k.tag("pe")
                        w = [("pe", mmc)]
                        if sg_user:
                            w.append(("dv", sg_user))
                        c_si = k.step("ac", lambda e, tp=tp: e.activation(
                            sg[:, :], PG[tp][:, :], AF.Silu), waits=w, inc="ac")
                        pg_free[tp] = c_si
                        w = [("ac", c_si)]
                        if tc == 0 and hs_user[par]:
                            w.append(("gp", hs_user[par]))
                        hm = k.step(
                            "dv", lambda e, tc=tc, par=par, tp=tp: e.tensor_mul(
                                hstg[par][:, tc * 512:(tc + 1) * 512], sg[:, :],
                                PU[tp][:, :]), waits=w, inc="dv")
                        pu_free[tp] = hm
                        sg_user = hm
                    hs_user[par] = k.dma("gp", lambda e, it=it, par=par: e.dma_start(
                        hT_s[it, :, :], hstg[par][:, :]),
                        waits=[("dv", pu_free[3 % 2])])
                k.barrier()
                k.flush()

        if phases >= 6:
            with contextlib.ExitStack() as p6:
                dW = 2816
                d_sh = p6.enter_context(nc.sbuf_tensor("p6sh", [128, dW], I32))
                d_m3 = p6.enter_context(nc.sbuf_tensor("p6m3", [128, dW], I32))
                d_f8 = p6.enter_context(nc.sbuf_tensor("p6f8", [128, dW], I32))
                d_m3f = p6.enter_context(nc.sbuf_tensor("p6m3f", [128, dW], F32))
                d_r1 = p6.enter_context(nc.sbuf_tensor("p6r1", [128, dW], F32))
                d_r2 = p6.enter_context(nc.sbuf_tensor("p6r2", [128, dW], F32))
                d_vv = p6.enter_context(nc.sbuf_tensor("p6vv", [128, dW], F32))
                d_tf = p6.enter_context(nc.sbuf_tensor("p6tf", [128, dW], F32))
                wrd = [p6.enter_context(nc.sbuf_tensor(f"p6wd{i}", [128, 704], I32))
                       for i in range(2)]
                scl = [p6.enter_context(nc.sbuf_tensor(f"p6sc{i}", [128, 44], F32))
                       for i in range(2)]
                wdo = [p6.enter_context(nc.sbuf_tensor(f"p6do{i}", [128, IFF], BF16))
                       for i in range(2)]
                wdT = p6.enter_context(nc.sbuf_tensor("p6dT", [128, NIT * 512], BF16))
                hsl = [p6.enter_context(nc.sbuf_tensor(f"p6h{i}", [128, NIT * 128], BF16))
                       for i in range(2)]
                yst = [p6.enter_context(nc.sbuf_tensor(f"p6y{i}", [128, 512], F32))
                       for i in range(2)]
                ydump = p6.enter_context(nc.sbuf_tensor("p6yd", [128, 512], F32))
                PY = [p6.enter_context(nc.psum_tensor(f"p6py{i}", [128, 512], F32))
                      for i in range(2)]
                PT = [p6.enter_context(nc.psum_tensor(f"p6pt{i}", [128, 512], BF16))
                      for i in range(2)]

                PW = 1408

                def deq6_front(words, p, waits=()):
                    wc = 176 * p
                    sc = PW * (p % 2)
                    dst = d_sh[:, sc:sc + PW].rearrange("p (w j) -> p w j", j=8)
                    src = bass.AP(words, wc, [[704, 128], [1, 176], [0, 8]])
                    shv = bass.AP(shpb, 0, [[8, 128], [0, 176], [1, 8]])
                    k.step("dv", lambda e, d=dst, src=src, shv=shv:
                           e.tensor_tensor(d, src, shv, ALU.logical_shift_right),
                           waits=waits)
                    k.step("dv", lambda e, sc=sc: e.tensor_scalar(
                        d_m3[:, sc:sc + PW], d_sh[:, sc:sc + PW], 7, None,
                        ALU.bitwise_and))
                    k.step("dv", lambda e, sc=sc: e.tensor_scalar(
                        d_f8[:, sc:sc + PW], d_sh[:, sc:sc + PW], 8, None,
                        ALU.bitwise_and))
                    c1 = k.step("dv", lambda e, sc=sc: e.tensor_copy(
                        d_m3f[:, sc:sc + PW], d_m3[:, sc:sc + PW]), inc="dv")
                    k.step("ac", lambda e, sc=sc: e.activation(
                        d_r1[:, sc:sc + PW], d_m3f[:, sc:sc + PW], AF.Relu,
                        bias=c_m4), waits=[("dv", c1)])
                    a1 = k.step("ac", lambda e, sc=sc: e.activation(
                        d_r2[:, sc:sc + PW], d_m3f[:, sc:sc + PW], AF.Relu,
                        bias=c_m6), inc="ac")
                    a2 = k.step("ac", lambda e, sc=sc: e.activation(
                        d_tf[:, sc:sc + PW], d_f8[:, sc:sc + PW], AF.Copy,
                        bias=1.0, scale=-0.25), inc="ac")
                    return a1, a2

                def deq6_back(sclf, outb, p, acts, pw=()):
                    a1, a2 = acts
                    sc = PW * (p % 2)
                    oc = PW * p
                    k.step("dv", lambda e, sc=sc: e.tensor_add(
                        d_vv[:, sc:sc + PW], d_m3f[:, sc:sc + PW],
                        d_r1[:, sc:sc + PW]), waits=[("ac", a1)] + list(pw))
                    k.step("dv", lambda e, sc=sc: e.scalar_tensor_tensor(
                        d_vv[:, sc:sc + PW], d_r2[:, sc:sc + PW], 2.0,
                        d_vv[:, sc:sc + PW], ALU.mult, ALU.add))
                    scv = bass.AP(sclf, 11 * p, [[44, 128], [1, 11], [0, 128]])
                    k.step("dv", lambda e, scv=scv, sc=sc: e.tensor_tensor(
                        d_tf[:, sc:sc + PW].rearrange("p (g f) -> p g f", g=11),
                        d_tf[:, sc:sc + PW].rearrange("p (g f) -> p g f", g=11),
                        scv, ALU.mult), waits=[("ac", a2)])
                    return k.step("dv", lambda e, o=outb, oc=oc, sc=sc: e.tensor_mul(
                        o[:, oc:oc + PW], d_vv[:, sc:sc + PW], d_tf[:, sc:sc + PW]),
                        inc="dv")

                hld = {}
                py_free = [0, 0]
                yst_user = [(0, 0), (0, 0)]
                ysq_user = [0, 0]
                bk_dv = {}
                bb_all = {}
                tr6 = {}
                ev6 = {}
                ptfree = [0, 0]

                def emit_deq6(g):
                    ec, et = g // 4, g % 4
                    par = g % 2
                    r0 = ec * 512 + et * 128
                    w = [("w6", k.cnt["w6"])]
                    if g - 2 in bk_dv:
                        w.append(("dv", bk_dv[g - 2]))
                    k.dma("gp", lambda e, r0=r0, par=par: e.dma_start(
                        wrd[par][:, :], dnp[r0:r0 + 128, :]), waits=w, sem="w6")
                    k.dma("gp", lambda e, r0=r0, par=par: e.dma_start(
                        scl[par][:, :], dns[r0:r0 + 128, :]), sem="w6")
                    c_l = k.cnt["w6"]
                    c_h = k.step("dv", lambda e, par=par: e.tensor_scalar(
                        scl[par][:, :], scl[par][:, :], 0.5, None, ALU.mult),
                        waits=[("w6", c_l)], inc="dv")
                    pw = [("pe", tr6[g - 2])] if g - 2 in tr6 else []
                    bb = {}
                    for pp_ in (0, 2):
                        aa = [deq6_front(wrd[par], pp_, waits=[("dv", c_h)]
                                         if pp_ == 0 else ()),
                              deq6_front(wrd[par], pp_ + 1)]
                        bb[pp_] = deq6_back(scl[par], wdo[par], pp_, aa[0], pw)
                        bb[pp_ + 1] = deq6_back(scl[par], wdo[par], pp_ + 1, aa[1], pw)
                    bk_dv[g] = bb[3]
                    bb_all[g] = bb

                def emit_trans6(g):
                    et = g % 4
                    par = g % 2
                    bb = bb_all[g]
                    for gi in range(11):
                        pp = gi % 2
                        pend = min(3, (gi * 4 + 3) * 128 // PW)
                        for j in range(4):
                            ic = gi * 4 + j
                            w = [("dv", bb[pend])]
                            if j == 0 and ptfree[pp]:
                                w.append(("ac", ptfree[pp]))
                            k.step("pe", lambda e, ic=ic, pp=pp, j=j, par=par:
                                   e.transpose(PT[pp][:, j * 128:(j + 1) * 128],
                                               wdo[par][:, ic * 128:(ic + 1) * 128],
                                               identb[:, :]), waits=w)
                        c_tr = k.tag("pe")
                        dst = bass.AP(wdT, (gi * 4) * 512 + et * 128,
                                      [[NIT * 512, 128], [512, 4], [1, 128]])
                        ptfree[pp] = k.step(
                            "ac", lambda e, pp=pp, dst=dst: e.copy(
                                dst, PT[pp][:, :]), waits=[("pe", c_tr)], inc="ac")
                    tr6[g] = k.cnt["pe"]
                    ev6[g] = ptfree[pp]

                _P6PF = bool(int(os.environ.get("MK_P6PF", "1")))
                emit_deq6(0)
                emit_deq6(1)
                for ec in range(4):
                    if 4 * ec not in bb_all:
                        emit_deq6(4 * ec)
                        emit_deq6(4 * ec + 1)
                    emit_trans6(4 * ec + 0)
                    emit_trans6(4 * ec + 1)
                    emit_deq6(4 * ec + 2)
                    emit_deq6(4 * ec + 3)
                    emit_trans6(4 * ec + 2)
                    emit_trans6(4 * ec + 3)
                    if ec < 3 and _P6PF:
                        emit_deq6(4 * ec + 4)
                        emit_deq6(4 * ec + 5)
                    wd_ready = ev6[4 * ec + 3]
                    if dbg:
                        k.dma("gp", lambda e, ec=ec: e.dma_start(
                            dbg_wdT[ec, :, :], wdT[:, :]),
                            waits=[("ac", wd_ready)])
                    if ec == 0:
                        hld[0] = k.dma("sy", lambda e: e.dma_start(
                            hsl[0][:, :].rearrange("p (ic c) -> p ic c", ic=NIT),
                            bass.AP(hT_s, 0, [[TOWN, 128], [128 * TOWN, NIT],
                                              [1, 128]])))
                    for t in range(NTO):
                        par = t % 2
                        if t + 1 < NTO or ec < 3:
                            nt_ = t + 1 if t + 1 < NTO else 0
                            npar = (t + 1) % 2
                            off = nt_ * 128
                            w = [("pe", k.cnt["pe"])] if True else []
                            hld[nt_] = k.dma(
                                "sy", lambda e, off=off, npar=npar: e.dma_start(
                                    hsl[npar][:, :].rearrange(
                                        "p (ic c) -> p ic c", ic=NIT),
                                    bass.AP(hT_s, off,
                                            [[TOWN, 128], [128 * TOWN, NIT],
                                             [1, 128]])), waits=w)
                        for ic in range(NIT):
                            w = []
                            if ic == 0:
                                w = [("ac", wd_ready), ("sy", hld[t])]
                                if py_free[par]:
                                    w.append(("dv", py_free[par]))
                            k.step("pe", lambda e, ic=ic, t=t, par=par: e.matmul(
                                PY[par][:, :],
                                hsl[par][:, ic * 128:(ic + 1) * 128],
                                wdT[:, ic * 512:(ic + 1) * 512],
                                start=(ic == 0), stop=(ic == NIT - 1)), waits=w)
                        mmc = k.tag("pe")
                        w = [("pe", mmc)]
                        if yst_user[par][0]:
                            w.append(("gp", yst_user[par][0]))
                        if yst_user[par][1]:
                            w.append(("ac", yst_user[par][1]))
                        py_free[par] = k.step("dv", lambda e, par=par: e.tensor_copy(
                            yst[par][:, :], PY[par][:, :]), waits=w, inc="dv")
                        ydma = k.dma("gp", lambda e, t=t, ec=ec, par=par: e.dma_start(
                            y_s[t * 128:(t + 1) * 128, ec * 512:(ec + 1) * 512],
                            yst[par][:, :]), waits=[("dv", py_free[par])])
                        w = [("dv", py_free[par])]
                        if ysq_user[par]:
                            w.append(("dv", ysq_user[par]))
                        c_sq = k.step("ac", lambda e, par=par: e.activation(
                            ydump[:, :], yst[par][:, :], AF.Square,
                            accum_out=small[:, 8 + par:9 + par]), waits=w, inc="ac")
                        yst_user[par] = (ydma, c_sq)
                        if ec == 0:
                            ysq_user[par] = k.step(
                                "dv", lambda e, t=t, par=par: e.tensor_copy(
                                    ysq[:, t:t + 1], small[:, 8 + par:9 + par]),
                                waits=[("ac", c_sq)], inc="dv")
                        else:
                            ysq_user[par] = k.step(
                                "dv", lambda e, t=t, par=par: e.tensor_add(
                                    ysq[:, t:t + 1], ysq[:, t:t + 1],
                                    small[:, 8 + par:9 + par]),
                                waits=[("ac", c_sq)], inc="dv")
                k.barrier()
                k.flush()

        if phases >= 7:
            with contextlib.ExitStack() as p7:
                yt = [p7.enter_context(nc.sbuf_tensor(f"p7y{i}", [128, E], F32))
                      for i in range(2)]
                x1t = [p7.enter_context(nc.sbuf_tensor(f"p7x{i}", [128, E], F32))
                       for i in range(2)]
                ot = [p7.enter_context(nc.sbuf_tensor(f"p7o{i}", [128, E], F32))
                      for i in range(2)]
                ot_user = [0, 0]
                rr_user = [0, 0]
                for t in range(NTO):
                    par = t % 2
                    w = [("dv", ot_user[par])] if ot_user[par] else []
                    k.dma("sy", lambda e, t=t, par=par: e.dma_start(
                        yt[par][:, :], y_s[t * 128:(t + 1) * 128, :]), waits=w)
                    c_l = k.dma("sy", lambda e, t=t, par=par: e.dma_start(
                        x1t[par][:, :], x1_s[t * 128:(t + 1) * 128, :]))
                    w = [("dv", rr_user[par])] if rr_user[par] else []
                    c_ln = k.step("ac", lambda e, t=t, par=par: e.activation(
                        small[:, 2 + 2 * par:3 + 2 * par], ysq[:, t:t + 1], AF.Ln,
                        bias=c_erms, scale=1.0 / E), waits=w, inc="ac")
                    c_r = k.step("ac", lambda e, par=par: e.activation(
                        small[:, 3 + 2 * par:4 + 2 * par],
                        small[:, 2 + 2 * par:3 + 2 * par], AF.Exp, scale=-0.5),
                        waits=[("ac", c_ln)], inc="ac")
                    c_m = k.step("dv", lambda e, par=par: e.tensor_scalar(
                        ot[par][:, :], yt[par][:, :], small[:, 3 + 2 * par:4 + 2 * par],
                        None, ALU.mult),
                        waits=[("ac", c_r), ("sy", c_l)], inc="dv")
                    rr_user[par] = c_m
                    c_o = k.step("dv", lambda e, par=par: e.tensor_add(
                        ot[par][:, :], ot[par][:, :], x1t[par][:, :]),
                        waits=[("dv", c_m)], inc="dv")
                    ot_user[par] = c_o
                    k.dma("sy", lambda e, t=t, par=par: e.dma_start(
                        out[t * 128:(t + 1) * 128, :], ot[par][:, :]),
                        waits=[("dv", c_o)], sem="st")
                k.step("sy", lambda e: e.nop(), waits=[("st", k.cnt["st"])])
                k.flush()
        k.flush()
    return nc


_ROPE_CACHE = {}


def _rope_tables():
    if "t" not in _ROPE_CACHE:
        inv = 1.0 / 10000.0 ** (np.arange(0, 64, 2)[:32].astype(np.float64) / 64)
        ang = np.outer(np.arange(T), inv)
        c = np.cos(ang)
        s = np.sin(ang)
        scl = 128.0 ** 0.25
        idx = (np.arange(E) % 64) // 2
        _ROPE_CACHE["t"] = ((c[:, idx] * scl).astype(np.float32),
                            (s[:, idx] * scl).astype(np.float32))
    return _ROPE_CACHE["t"]


def kernel(L0_attn_x, L0_attn_wq, L0_attn_wk, L0_attn_wv, L0_attn_wout,
           L0_attn_lambda_q1, L0_attn_lambda_q2, L0_attn_lambda_k1,
           L0_attn_lambda_k2, L0_attn_subln_weight,
           L0_ffn_gate_up_packed, L0_ffn_gate_up_scales,
           L0_ffn_down_packed, L0_ffn_down_scales):
    x = np.asarray(L0_attn_x)
    ropeC, ropeS = _rope_tables()
    lvec = np.concatenate([
        np.asarray(L0_attn_lambda_q1), np.asarray(L0_attn_lambda_k1),
        np.asarray(L0_attn_lambda_q2), np.asarray(L0_attn_lambda_k2)],
        axis=0).reshape(1, 256).astype(np.float32)
    common = dict(
        wqT=np.ascontiguousarray(np.asarray(L0_attn_wq).T),
        wkT=np.ascontiguousarray(np.asarray(L0_attn_wk).T),
        wvT=np.ascontiguousarray(np.asarray(L0_attn_wv).T),
        woT=np.ascontiguousarray(np.asarray(L0_attn_wout).T),
        lvec=lvec,
        swn=np.asarray(L0_attn_subln_weight).reshape(128, 1).astype(np.float32),
        gup=np.asarray(L0_ffn_gate_up_packed),
        gus=np.ascontiguousarray(np.asarray(L0_ffn_gate_up_scales).T),
        dnp=np.asarray(L0_ffn_down_packed),
        dns=np.ascontiguousarray(np.asarray(L0_ffn_down_scales).T),
        ident=np.eye(128, dtype=ml_dtypes.bfloat16),
        shp=np.tile((np.arange(8, dtype=np.int32) * 4)[None, :], (128, 1)),
    )
    in_maps = []
    for c in range(8):
        b, hh = c // 2, c % 2
        if hh == 0:
            perm_x = x[b]
            rc, rs = ropeC, ropeS
        else:
            perm_x = np.concatenate([x[b][TOWN:], x[b][:TOWN]], axis=0)
            rc = np.concatenate([ropeC[TOWN:], ropeC[:TOWN]], axis=0)
            rs = np.concatenate([ropeS[TOWN:], ropeS[:TOWN]], axis=0)
        m = dict(common)
        m["xb"] = np.ascontiguousarray(perm_x)
        m["ropeC"] = np.ascontiguousarray(rc)
        m["ropeS"] = np.ascontiguousarray(rs)
        in_maps.append(m)

    nc = build(phases=_PHASES, dbg=_DBG)
    res = run_bass_kernel_spmd(nc, in_maps, core_ids=list(range(8)))
    out = np.empty((4, T, E), dtype=np.float32)
    for c in range(8):
        b, hh = c // 2, c % 2
        out[b, hh * TOWN:(hh + 1) * TOWN] = res.results[c]["out"]
    if _DBG:
        kernel.last_results = res.results
    return out
